# version 15
# baseline (speedup 1.0000x reference)
import numpy as np
import ml_dtypes
from contextlib import ExitStack

import concourse.bass as bass
import concourse.mybir as mybir
from concourse.bass_utils import run_bass_kernel_spmd

F32 = mybir.dt.float32
BF16 = mybir.dt.bfloat16
ALU = mybir.AluOpType
AF = mybir.ActivationFunctionType
AX = mybir.AxisListType.X

LN_EPS = 1e-5
Q_EPS = 1e-8
TC = 128

ENGINES = ("sync", "scalar", "vector", "tensor", "gpsimd")

CFG_FULL = dict(B_loc=2, T=2048, D=2048, H=5632, S=512)


class Op:
    __slots__ = ("engine", "emit", "deps", "sem", "inc_n", "val")

    def __init__(self, engine, emit, deps, sem, inc_n):
        self.engine = engine
        self.emit = emit
        self.deps = deps
        self.sem = sem
        self.inc_n = inc_n
        self.val = None


class Reg:
    __slots__ = ("w", "readers")

    def __init__(self):
        self.w = None
        self.readers = []


class Sched:
    def __init__(self, sems, dma_sems):
        self.sems = sems
        self.dma_sems = dma_sems
        self.ops = []
        self.regs = {}

    def reg(self, name):
        if name not in self.regs:
            self.regs[name] = Reg()
        return self.regs[name]

    def rdeps(self, *names):
        d = []
        for n in names:
            r = self.reg(n)
            if r.w is not None:
                d.append(r.w)
        return d

    def wdeps(self, *names):
        d = []
        for n in names:
            r = self.reg(n)
            d.extend(r.readers)
            if r.w is not None:
                d.append(r.w)
        return d

    def note_read(self, op, *names):
        for n in names:
            self.reg(n).readers.append(op)

    def wset(self, op, *names):
        for n in names:
            r = self.reg(n)
            r.w = op
            r.readers = []

    def add(self, engine, emit, deps=(), sem="auto"):
        if sem == "auto":
            s, n = self.sems[engine], 1
        elif sem is None:
            s, n = None, 0
        else:
            s, n = sem
        deps = [d for d in deps if d is not None]
        op = Op(engine, emit, deps, s, n)
        self.ops.append(op)
        return op

    def dma(self, queue, dst, src, deps=(), key=None):
        s = self.dma_sems[key if key is not None else queue]
        return self.add(queue, lambda e, d=dst, s_=src: e.dma_start(d, s_),
                        deps=deps, sem=(s, 16))

    def wait_only(self, engine, deps):
        return self.add(engine, lambda e: None, deps=deps, sem=None)

    def finalize(self):
        counts = {}
        for op in self.ops:
            if op.sem is not None:
                k = id(op.sem)
                counts[k] = counts.get(k, 0) + op.inc_n
                op.val = counts[k]

    def emit_all(self, block):
        for eng_name in ENGINES:
            ops = [o for o in self.ops if o.engine == eng_name]
            if not ops:
                continue

            def body(eng, ops=ops):
                waited = {}
                for op in ops:
                    for d in op.deps:
                        assert d.sem is not None, "dep on op without sem"
                        k = id(d.sem)
                        if waited.get(k, -1) >= d.val:
                            continue
                        eng.wait_ge(d.sem, d.val)
                        waited[k] = d.val
                    inst = op.emit(eng)
                    if op.sem is not None:
                        inst.then_inc(op.sem, op.inc_n)

            getattr(block, eng_name)(body)


def build_program(nc: bass.Bass, cfg):
    B_loc, T, D, H, S = (cfg[k] for k in ("B_loc", "T", "D", "H", "S"))
    NTC = S // TC
    NDC = D // 128
    NJC = H // 128
    JBW = min(H, 512)
    NJB = H // JBW
    NJC_IN = JBW // 128
    EBW = min(D, 512)
    NEB = D // EBW
    NT_SEQ = T // S
    NTILE = B_loc * NT_SEQ
    K = 4
    assert S % TC == 0 and D % 128 == 0 and H % JBW == 0 and D % EBW == 0

    max_g = max(1, 8192 // EBW)
    pb_parts = []
    r = NJC
    while r > 0:
        g = min(max_g, r)
        pb_parts.append(g)
        r -= g

    x_in = nc.dram_tensor("x", [B_loc, T, D], F32, kind="ExternalInput").ap()
    Wt_in = nc.dram_tensor("Wt", [D, D], F32, kind="ExternalInput").ap()
    lam_in = nc.dram_tensor("lam_logit", [K], F32, kind="ExternalInput").ap()
    eta_in = nc.dram_tensor("eta", [K], F32, kind="ExternalInput").ap()
    gamma_in = nc.dram_tensor("gamma", [D], F32, kind="ExternalInput").ap()
    beta_in = nc.dram_tensor("beta", [D], F32, kind="ExternalInput").ap()
    w1t_in = nc.dram_tensor("w1t", [D, H], F32, kind="ExternalInput").ap()
    w2t_in = nc.dram_tensor("w2t", [D, H], F32, kind="ExternalInput").ap()
    w3t_in = nc.dram_tensor("w3t", [H, D], F32, kind="ExternalInput").ap()
    y_out = nc.dram_tensor("y", [B_loc, T, D], F32, kind="ExternalOutput").ap()

    q1s = nc.dram_tensor("q1s", [D, H], BF16, kind="Internal").ap()
    q2s = nc.dram_tensor("q2s", [D, H], BF16, kind="Internal").ap()
    q3s = nc.dram_tensor("q3s", [H, D], BF16, kind="Internal").ap()
    wts = nc.dram_tensor("wts", [D, D], BF16, kind="Internal").ap()

    with ExitStack() as ctx:
        def sbuf(name, shape, dt=F32):
            return ctx.enter_context(nc.sbuf_tensor(name, shape, dt))

        def psum_t(name, shape, dt=F32):
            return ctx.enter_context(nc.psum_tensor(name, shape, dt))

        def sem(name):
            return ctx.enter_context(nc.semaphore(name))

        sems = {e: sem("s_" + e) for e in ENGINES}
        dma_keys = ("par", "sg0", "sg1", "sg2", "sg3", "wl0", "wl1",
                    "r0", "r1", "xl0", "xl1", "ys0", "ys1", "tl0", "tl1")
        dma_sems = {k: sem("sd_" + k) for k in dma_keys}
        sc = Sched(sems, dma_sems)

        pT = psum_t("pT", [128, 1024], BF16)

        Gbc = sbuf("Gbc", [128, D])
        Bbc = sbuf("Bbc", [128, D])
        carry = sbuf("carry", [K, D])
        carry_bf = sbuf("carry_bf", [K, D], BF16)
        LtT = sbuf("LtT", [TC, TC], BF16)
        CPb = sbuf("CPb", [K, TC], BF16)
        CUb = sbuf("CUb", [TC, K], BF16)
        lam128 = sbuf("lam128", [K, 1])
        ident = sbuf("ident", [128, 128], BF16)
        ones1f = sbuf("ones1f", [1, 128])
        onescolf = sbuf("onescolf", [128, 1])
        stats = sbuf("stats", [128, 48])
        stats2 = sbuf("stats2", [128, 32])
        recip = sbuf("recip", [128, 3])
        cEps = sbuf("cEps", [128, 1])
        cTC = sbuf("cTC", [128, 1])

        scratch_done = {}

        xb0_t = sbuf("xb0", [128, (S // TC) * D])
        ubuf4 = sbuf("ubuf4", [128, (S // TC) * D], BF16)
        ufm = sbuf("ufm", [128, (D // 128) * S], BF16)

        MX = max(D, H)
        with nc.sbuf_tensor("pro_w0", [128, MX], F32) as pro_w0, \
             nc.sbuf_tensor("pro_w1", [128, MX], F32) as pro_w1, \
             nc.sbuf_tensor("pro_qa", [128, MX], F32) as pro_qa, \
             nc.sbuf_tensor("pro_qb", [128, MX], F32) as pro_qb, \
             nc.sbuf_tensor("pro_qc", [128, MX], BF16) as pro_qc, \
             nc.sbuf_tensor("pro_m", [128, 1024], F32) as pro_m, \
             nc.psum_tensor("pPro", [128, 2048], F32) as pPro:

            LAMR, ETAR, LNLR = 0, 4, 8
            LAMC, LNLC, ETAC = 12, 13, 14
            LNL128, ETA128 = 16, 20
            ICOL, ICOLR, RACC, GTMP, GTMP2, CUS = 24, 25, 26, 27, 28, 29
            M0, TS, MSK, ACC, EK, TK, IROW = 32, 160, 288, 416, 544, 672, 800

            sc.dma("scalar", pro_m[0:1, LAMR:LAMR + K], lam_in[None, :],
                   key="par")
            sc.dma("scalar", pro_m[0:1, ETAR:ETAR + K], eta_in[None, :],
                   key="par")
            sc.dma("scalar", pro_m[0:K, LAMC:LAMC + 1], lam_in[:, None],
                   key="par")
            sc.dma("scalar", pro_m[0:K, ETAC:ETAC + 1], eta_in[:, None],
                   key="par")
            sc.dma("scalar", pro_w0[0:1, 0:D], gamma_in[None, :], key="par")
            d_params = sc.dma("scalar", pro_w1[0:1, 0:D], beta_in[None, :],
                              key="par")
            d_lam_r = d_eta_r = d_lam_c = d_eta_c = d_gam = d_bet = d_params

            g_ic = sc.add("gpsimd", lambda e: e.iota(
                pro_m[:, ICOL:ICOL + 1], [[0, 1]], channel_multiplier=1,
                allow_small_or_imprecise_dtypes=True))
            g_icr = sc.add("gpsimd", lambda e: e.iota(
                pro_m[:, ICOLR:ICOLR + 1], [[0, 1]], base=TC - 1,
                channel_multiplier=-1, allow_small_or_imprecise_dtypes=True))
            g_ir = sc.add("gpsimd", lambda e: e.iota(
                pro_m[0:1, IROW:IROW + 128], [[1, 128]], channel_multiplier=0,
                allow_small_or_imprecise_dtypes=True))
            g_o1 = sc.add("gpsimd", lambda e: e.memset(ones1f[:], 1.0))
            g_oc = sc.add("gpsimd", lambda e: e.memset(onescolf[:], 1.0))
            g_c0 = sc.add("gpsimd", lambda e: e.memset(carry[:], 0.0))
            g_c1 = sc.add("gpsimd", lambda e: e.memset(carry_bf[:], 0.0))
            g_ce = sc.add("gpsimd", lambda e: e.memset(cEps[:], LN_EPS))
            g_ct = sc.add("gpsimd", lambda e: e.memset(cTC[:], float(TC)))
            sc.wset(g_ce, "cEps")
            sc.wset(g_ct, "cTC")
            sc.wset(g_c0, "carry")
            sc.wset(g_c1, "carry_bf")

            a_sr = sc.add("scalar", lambda e: e.activation(
                pro_m[0:1, LAMR:LAMR + K], pro_m[0:1, LAMR:LAMR + K],
                AF.Sigmoid), deps=[d_lam_r])
            a_sc = sc.add("scalar", lambda e: e.activation(
                pro_m[0:K, LAMC:LAMC + 1], pro_m[0:K, LAMC:LAMC + 1],
                AF.Sigmoid), deps=[d_lam_c])
            a_lr = sc.add("scalar", lambda e: e.activation(
                pro_m[0:1, LNLR:LNLR + K], pro_m[0:1, LAMR:LAMR + K], AF.Ln),
                deps=[a_sr])
            a_lc = sc.add("scalar", lambda e: e.activation(
                pro_m[0:K, LNLC:LNLC + 1], pro_m[0:K, LAMC:LAMC + 1], AF.Ln),
                deps=[a_sc])
            a_l128 = sc.add("scalar", lambda e: e.activation(
                lam128[:, 0:1], pro_m[0:K, LNLC:LNLC + 1], AF.Exp,
                scale=cTC[0:K, 0:1]), deps=[a_lc, g_ct])
            sc.wset(a_l128, "lam128")

            def bcast(dst_ap, src_row_ap, width, deps):
                t = None
                first_deps = list(deps) + [g_o1] + sc.wdeps("pPro")
                for c0 in range(0, width, 512):
                    cw = min(512, width - c0)
                    t = sc.add("tensor",
                               lambda e, s=src_row_ap, c0=c0, cw=cw: e.matmul(
                                   pPro[:, c0:c0 + cw], ones1f[:],
                                   s[:, c0:c0 + cw], start=True, stop=True),
                               deps=first_deps if c0 == 0 else [],
                               sem=None if c0 + cw < width else "auto")
                sc.wset(t, "pPro")
                v = sc.add("vector",
                           lambda e, d=dst_ap, w=width: e.tensor_copy(
                               d, pPro[:, 0:w]), deps=[t])
                sc.note_read(v, "pPro")
                return v

            v_m0 = bcast(pro_m[:, M0:M0 + 128], pro_m[0:1, IROW:IROW + 128],
                         128, [g_ir])
            v_l128 = bcast(pro_m[:, LNL128:LNL128 + K],
                           pro_m[0:1, LNLR:LNLR + K], K, [a_lr])
            v_e128 = bcast(pro_m[:, ETA128:ETA128 + K],
                           pro_m[0:1, ETAR:ETAR + K], K, [d_eta_r])
            v_gbc = bcast(Gbc[:], pro_w0[0:1, 0:D], D, [d_gam])
            sc.wset(v_gbc, "Gbc")
            v_bbc = bcast(Bbc[:], pro_w1[0:1, 0:D], D, [d_bet])
            sc.wset(v_bbc, "Bbc")

            v_ts = sc.add("vector", lambda e: e.tensor_scalar(
                pro_m[:, TS:TS + 128], pro_m[:, M0:M0 + 128],
                pro_m[:, ICOL:ICOL + 1], None, ALU.subtract),
                deps=[v_m0, g_ic])
            v_msk = sc.add("vector", lambda e: e.tensor_single_scalar(
                pro_m[:, MSK:MSK + 128], pro_m[:, TS:TS + 128], 0.0,
                ALU.is_ge), deps=[v_ts])
            v_eq = sc.add("vector", lambda e: e.tensor_single_scalar(
                pro_m[:, ACC:ACC + 128], pro_m[:, TS:TS + 128], 0.0,
                ALU.is_equal), deps=[v_msk])
            v_id = sc.add("vector", lambda e: e.tensor_copy(
                ident[:], pro_m[:, ACC:ACC + 128]), deps=[v_eq])
            sc.wset(v_id, "ident")
            v_tsc = sc.add("vector", lambda e: e.tensor_scalar_max(
                pro_m[:, TS:TS + 128], pro_m[:, TS:TS + 128], 0.0),
                deps=[v_eq, v_id])

            last = v_tsc
            for k in range(K):
                ek = sc.add("scalar", lambda e, k=k: e.activation(
                    pro_m[:, EK:EK + 128], pro_m[:, TS:TS + 128], AF.Exp,
                    scale=pro_m[:, LNL128 + k:LNL128 + k + 1]),
                    deps=[last, v_l128, v_tsc, a_l128])
                tk = sc.add("vector", lambda e, k=k: e.scalar_tensor_tensor(
                    pro_m[:, TK:TK + 128], pro_m[:, EK:EK + 128],
                    pro_m[:, ETA128 + k:ETA128 + k + 1],
                    pro_m[:, MSK:MSK + 128], ALU.mult, ALU.mult),
                    deps=[ek, v_e128, v_msk])
                last = sc.add("vector", lambda e: e.tensor_add(
                    pro_m[:, ACC:ACC + 128], pro_m[:, ACC:ACC + 128],
                    pro_m[:, TK:TK + 128]), deps=[tk])
            v_lt = sc.add("vector", lambda e: e.tensor_copy(
                LtT[:], pro_m[:, ACC:ACC + 128]), deps=[last])
            sc.wset(v_lt, "LtT")

            a_cp = sc.add("scalar", lambda e: e.activation(
                pro_m[0:K, EK:EK + TC], pro_m[0:K, M0:M0 + TC], AF.Exp,
                scale=pro_m[0:K, LNLC:LNLC + 1],
                bias=pro_m[0:K, LNLC:LNLC + 1]),
                deps=[a_lc, v_m0, last])
            v_cp1 = sc.add("vector", lambda e: e.tensor_scalar(
                pro_m[0:K, TK:TK + TC], pro_m[0:K, EK:EK + TC],
                pro_m[0:K, ETAC:ETAC + 1], None, ALU.mult),
                deps=[a_cp, d_eta_c, v_lt])
            v_cp = sc.add("vector", lambda e: e.tensor_copy(
                CPb[:], pro_m[0:K, TK:TK + TC]), deps=[v_cp1])
            sc.wset(v_cp, "CPb")

            a_cu = sc.add("scalar", lambda e: e.activation(
                pro_m[:, CUS:CUS + K], pro_m[:, LNL128:LNL128 + K], AF.Exp,
                scale=pro_m[:, ICOLR:ICOLR + 1]),
                deps=[v_l128, g_icr, a_cp])
            v_cu = sc.add("vector", lambda e: e.tensor_copy(
                CUb[:], pro_m[:, CUS:CUS + K]), deps=[a_cu, v_cp])
            sc.wset(v_cu, "CUb")

            x0_ld = sc.dma("scalar",
                           xb0_t[:].rearrange("p (tc d) -> p tc d", d=D),
                           x_in[0, 0:S, :].rearrange(
                               "(tc q) d -> q tc d", q=128),
                           deps=[], key="xl0")
            sc.wset(x0_ld, *[f"xb0_{t}" for t in range(S // TC)])

            mats = [
                ("q1s", w1t_in, D, H, q1s, True, 0, "sg0"),
                ("q2s", w2t_in, D, H, q2s, True, 1, "sg1"),
            ]
            wbufs = [pro_w0, pro_w1]
            buf_free = [[v_gbc], [v_bbc]]
            last_qc = None
            qc_free = v_cu

            for (mname, w_ap, NR, NCOL, scratch, doq, rcol, sgkey) in mats:
                nch = NR // 128
                gamma_ready = None
                if doq:
                    z0 = sc.add("gpsimd", lambda e: e.memset(
                        pro_m[:, RACC:RACC + 1], 0.0), deps=sc.wdeps("racc"))
                    sc.wset(z0, "racc")
                    prev = z0
                    for c in range(nch):
                        buf = wbufs[c % 2]
                        ld = sc.dma("sync", buf[:, 0:NCOL],
                                    w_ap[c * 128:(c + 1) * 128, :],
                                    deps=buf_free[c % 2], key=f"wl{c % 2}")
                        red = sc.add("vector", lambda e, b=buf, n=NCOL:
                                     e.tensor_reduce(
                                         pro_m[:, GTMP:GTMP + 1], b[:, 0:n],
                                         AX, ALU.add,
                                         apply_absolute_value=True),
                                     deps=[ld, prev])
                        prev = sc.add("vector", lambda e: e.tensor_add(
                            pro_m[:, RACC:RACC + 1], pro_m[:, RACC:RACC + 1],
                            pro_m[:, GTMP:GTMP + 1]), deps=[red])
                        buf_free[c % 2] = [red]
                    tg = sc.add("tensor", lambda e: e.matmul(
                        pPro[0:1, 300:301], pro_m[:, RACC:RACC + 1],
                        onescolf[:], start=True, stop=True),
                        deps=[prev, g_oc] + sc.wdeps("pPro"))
                    sc.wset(tg, "pPro")
                    sc.note_read(tg, "racc")
                    gm = sc.add("vector", lambda e, n=NR * NCOL:
                                e.tensor_scalar(
                                    pro_m[0:1, GTMP:GTMP + 1],
                                    pPro[0:1, 300:301], 1.0 / n, Q_EPS,
                                    ALU.mult, ALU.add), deps=[tg])
                    sc.note_read(gm, "pPro")
                    rc = sc.add("vector", lambda e: e.reciprocal(
                        pro_m[0:1, GTMP2:GTMP2 + 1],
                        pro_m[0:1, GTMP:GTMP + 1]), deps=[gm])
                    gamma_ready = bcast(recip[:, rcol:rcol + 1],
                                        pro_m[0:1, GTMP2:GTMP2 + 1], 1, [rc])

                for c in range(nch):
                    buf = wbufs[c % 2]
                    ld = sc.dma("sync", buf[:, 0:NCOL],
                                w_ap[c * 128:(c + 1) * 128, :],
                                deps=buf_free[c % 2], key=f"wl{c % 2}")
                    if doq:
                        qa = sc.add("vector",
                                    lambda e, b=buf, n=NCOL, rcol=rcol:
                                    e.tensor_scalar(
                                        pro_qa[:, 0:n], b[:, 0:n],
                                        recip[:, rcol:rcol + 1], 0.5,
                                        ALU.mult, ALU.is_ge),
                                    deps=[ld, gamma_ready, last_qc])
                        qb = sc.add("vector",
                                    lambda e, b=buf, n=NCOL, rcol=rcol:
                                    e.tensor_scalar(
                                        pro_qb[:, 0:n], b[:, 0:n],
                                        recip[:, rcol:rcol + 1], -0.5,
                                        ALU.mult, ALU.is_le),
                                    deps=[ld, gamma_ready, last_qc])
                        qc = sc.add("gpsimd", lambda e, n=NCOL: e.tensor_sub(
                            pro_qc[:, 0:n], pro_qa[:, 0:n], pro_qb[:, 0:n]),
                            deps=[qa, qb, qc_free])
                    else:
                        qc = sc.add("gpsimd", lambda e, b=buf, n=NCOL:
                                    e.tensor_copy(pro_qc[:, 0:n], b[:, 0:n]),
                                    deps=[ld, qc_free, last_qc])
                    last_qc = qc
                    buf_free[c % 2] = [qc]
                    st_op = sc.dma("scalar",
                                   scratch[c * 128:(c + 1) * 128, :],
                                   pro_qc[:, 0:NCOL],
                                   deps=[qc], key=sgkey)
                    qc_free = st_op
                    scratch_done[mname] = st_op

            pro_done = [scratch_done[m] for m in ("q1s", "q2s")]
            pro_done += [v_gbc, v_bbc, v_lt, v_cp, v_cu, a_l128, g_c0, g_c1,
                         qc_free, last_qc]


        xb = [xb0_t, sbuf("xb1", [128, NTC * D])]
        HIDW = max(NJC * S, NTC * D + 4 * D)
        hid = sbuf("hid", [128, HIDW], BF16)
        ring = sbuf("ring", [128, 16384], BF16)
        sig = sbuf("sig", [128, NJC_IN * S], BF16)
        hid_addr = nc.lookup_mls("hid").memorylocations[0].addr
        asc_slots = [
            nc.alloc_sbuf_tensor_at("asc0", [128, D], F32,
                                    offset=hid_addr + NTC * D * 2),
            nc.alloc_sbuf_tensor_at("asc1", [128, D], F32,
                                    offset=hid_addr + NTC * D * 2 + 4 * D),
        ]

        NSLOT = min(NTC, 4)
        pM = psum_t("pM", [128, 2048], F32)
        pA = psum_t("pA", [128, 1024], F32)
        pT2b = psum_t("pT2b", [128, 1024], BF16)

        RING_SLOT = 8192
        ring_state = {"idx": 0, "first": True}
        ring_cons = {}

        def ring_load(src_ap, cols, inner, extra_deps=(), queue="sync"):
            i = ring_state["idx"]
            ring_state["idx"] += 1
            deps = list(extra_deps)
            if ring_state["first"]:
                deps += pro_done
                ring_state["first"] = False
            if (i - 2) in ring_cons:
                deps.append(ring_cons[i - 2])
            dst = ring[:, (i % 2) * RING_SLOT:(i % 2) * RING_SLOT + cols]
            dst = dst.rearrange("p (a b) -> p a b", b=inner)
            op = sc.dma(queue, dst, src_ap, deps=deps, key=f"r{i % 2}")
            return i, op

        def ring_ap(i, cols):
            return ring[:, (i % 2) * RING_SLOT:(i % 2) * RING_SLOT + cols]

        q1v = q1s.rearrange("(dc p) j -> p dc j", p=128)
        q2v = q2s.rearrange("(dc p) j -> p dc j", p=128)
        q3v = q3s.rearrange("(jc p) e2 -> p jc e2", p=128)
        wtv = wts.rearrange("(dc p) e2 -> p dc e2", p=128)

        def z_sb(tc):
            return hid[:, tc * D:(tc + 1) * D]

        def emit_ln(x_src_ap, slot, xreg, tc, extra_dep=None):
            ub = ubuf4[:, tc * D:(tc + 1) * D]
            ureg = f"ub{tc}"
            c0 = slot * 8
            s1 = stats[:, c0 + 0:c0 + 1]
            s2 = stats[:, c0 + 1:c0 + 2]
            nmu = stats[:, c0 + 2:c0 + 3]
            mu2 = stats[:, c0 + 3:c0 + 4]
            var = stats[:, c0 + 4:c0 + 5]
            stdv = stats[:, c0 + 5:c0 + 6]
            rstd = stats[:, c0 + 6:c0 + 7]
            sreg = f"stats{slot}"
            dep_x = sc.rdeps(xreg) + ([extra_dep] if extra_dep else [])
            sq = sc.add("vector", lambda e, xa=x_src_ap, ub=ub:
                e.scalar_tensor_tensor(ub, xa, 1.0, xa, ALU.mult, ALU.mult,
                                       accum_out=s2),
                deps=dep_x + sc.wdeps(ureg) + sc.wdeps(sreg))
            sc.note_read(sq, xreg)
            red = sc.add("vector", lambda e, xa=x_src_ap: e.tensor_reduce(
                s1, xa, AX, ALU.add), deps=dep_x + [sq])
            sc.note_read(red, xreg)
            nm = sc.add("vector", lambda e: e.tensor_single_scalar(
                nmu, s1, -1.0 / D, ALU.mult), deps=[red])
            m2 = sc.add("vector", lambda e: e.tensor_mul(mu2, nmu, nmu),
                        deps=[nm])
            vv = sc.add("vector", lambda e: e.scalar_tensor_tensor(
                var, s2, 1.0 / D, mu2, ALU.mult, ALU.subtract),
                deps=[m2, sq])
            sd = sc.add("scalar", lambda e: e.activation(
                stdv, var, AF.Sqrt, bias=cEps[:, 0:1]),
                deps=[vv] + sc.rdeps("cEps"))
            rs = sc.add("vector", lambda e: e.reciprocal(rstd, stdv),
                        deps=[sd])
            ap = sc.add("vector", lambda e, xa=x_src_ap, ub=ub: e.tensor_scalar(
                ub, xa, nmu, rstd, ALU.add, ALU.mult), deps=[rs, sq])
            sc.note_read(ap, xreg)
            sc.wset(ap, ureg)
            sc.wset(ap, sreg)
            return ap

        tpar = [0]

        def emit_transposes(tc, dep_apply):
            last_ev = None
            for f0 in range(0, NDC, 8):
                npc = min(8, NDC - f0)
                par = tpar[0] % 2
                tpar[0] += 1
                pt = pT if par == 0 else pT2b
                preg = "pTa" if par == 0 else "pTb"
                trs = []
                for i in range(npc):
                    tr = sc.add("tensor", lambda e, dc=f0 + i, i=i, tc=tc,
                                pt=pt:
                                e.transpose(
                                    pt[:, i * 128:(i + 1) * 128],
                                    ubuf4[:, tc * D + dc * 128:
                                          tc * D + (dc + 1) * 128],
                                    ident[:]),
                                deps=([dep_apply] + sc.rdeps("ident")
                                      + sc.wdeps(preg)) if i == 0 else [],
                                sem=None if i < npc - 1 else "auto")
                    trs.append(tr)
                sc.wset(trs[-1], preg)
                sc.note_read(trs[-1], f"ub{tc}")

                def mk(e, f0=f0, npc=npc, tc=tc, pt=pt):
                    dst = ufm[:].rearrange("p (dc t) -> p dc t", t=S)[
                        :, f0:f0 + npc, tc * 128:(tc + 1) * 128]
                    src = pt[:, 0:npc * 128].rearrange("p (a b) -> p a b",
                                                       b=128)
                    return e.tensor_copy(dst, src)

                ev = sc.add("vector", mk, deps=[trs[-1]] + sc.wdeps("ufm_w"))
                sc.note_read(ev, preg)
                last_ev = ev
            return last_ev

        def emit_ln_fin(xtile, xreg, tc, parts):
            ub = ubuf4[:, tc * D:(tc + 1) * D]
            ureg = f"ub{tc}"
            slot = tc % NSLOT
            c0 = slot * 8
            s1 = stats[:, c0 + 0:c0 + 1]
            s2 = stats[:, c0 + 1:c0 + 2]
            nmu = stats[:, c0 + 2:c0 + 3]
            mu2 = stats[:, c0 + 3:c0 + 4]
            var = stats[:, c0 + 4:c0 + 5]
            stdv = stats[:, c0 + 5:c0 + 6]
            rstd = stats[:, c0 + 6:c0 + 7]
            sreg = f"stats{slot}"
            f1 = sc.add("vector", lambda e, tc=tc: e.tensor_reduce(
                s1, stats2[:, tc * 8:tc * 8 + NEB], AX, ALU.add),
                deps=parts + sc.wdeps(sreg))
            f2 = sc.add("vector", lambda e, tc=tc: e.tensor_reduce(
                s2, stats2[:, tc * 8 + 4:tc * 8 + 4 + NEB], AX, ALU.add),
                deps=[f1])
            nm = sc.add("vector", lambda e: e.tensor_single_scalar(
                nmu, s1, -1.0 / D, ALU.mult), deps=[f1])
            m2 = sc.add("vector", lambda e: e.tensor_mul(mu2, nmu, nmu),
                        deps=[nm])
            vv = sc.add("vector", lambda e: e.scalar_tensor_tensor(
                var, s2, 1.0 / D, mu2, ALU.mult, ALU.subtract),
                deps=[m2, f2])
            sd = sc.add("scalar", lambda e: e.activation(
                stdv, var, AF.Sqrt, bias=cEps[:, 0:1]),
                deps=[vv] + sc.rdeps("cEps"))
            rs = sc.add("vector", lambda e: e.reciprocal(rstd, stdv),
                        deps=[sd])
            xa = xtile[:, tc * D:(tc + 1) * D]
            ap = sc.add("vector", lambda e, xa=xa, ub=ub: e.tensor_scalar(
                ub, xa, nmu, rstd, ALU.add, ALU.mult),
                deps=[rs] + sc.rdeps(xreg[tc]) + sc.wdeps(ureg))
            sc.note_read(ap, xreg[tc])
            sc.wset(ap, ureg)
            sc.wset(ap, sreg)
            return ap

        def emit_ln_stats_all(xtile, xreg, extra_dep=None):
            return [emit_ln(xtile[:, tc * D:(tc + 1) * D], tc % NSLOT,
                            xreg[tc], tc, extra_dep) for tc in range(NTC)]

        def emit_transposes_all(applies):
            last_ev = None
            for tc in range(NTC):
                last_ev = emit_transposes(tc, applies[tc])
            sc.wset(last_ev, "ufm_w")
            return last_ev

        def emit_ln_all(xtile, xreg, extra_dep=None):
            return emit_transposes_all(emit_ln_stats_all(xtile, xreg,
                                                         extra_dep))

        def ffn_sublayer(xtile, xreg, ufm_dep, ln_hook=None, collect=False,
                         stat_parts=None, pump=0):
            sig_ops = {}
            last_mm = None
            last_hm = None
            for jb in range(NJB):
                if pump:
                    tail_pump(pump)
                i1, ld1 = ring_load(
                    q1v[:, :, jb * JBW:(jb + 1) * JBW], NDC * JBW, JBW,
                    extra_deps=[scratch_done["q1s"]])
                i2, ld2 = ring_load(
                    q2v[:, :, jb * JBW:(jb + 1) * JBW], NDC * JBW, JBW,
                    extra_deps=[scratch_done["q2s"]])
                for half in range(2):
                    fi, ldop = (i1, ld1) if half == 0 else (i2, ld2)
                    wap = ring_ap(fi, NDC * JBW)
                    for jci in range(NJC_IN):
                        jc = jb * NJC_IN + jci
                        bsel = jc % 2
                        ssel = jci
                        pb = (pM[:, bsel * 512:bsel * 512 + S] if half == 0
                              else pA[:, bsel * 512:bsel * 512 + S])
                        breg = (f"pM{bsel}" if half == 0 else f"pA{bsel}")
                        mmdeps = [ldop, ufm_dep] + sc.wdeps(breg)
                        lm = None
                        for dc in range(NDC):
                            lm = sc.add(
                                "tensor",
                                lambda e, dc=dc, jci=jci, wap=wap, pb=pb:
                                e.matmul(
                                    pb,
                                    wap[:, dc * JBW + jci * 128:
                                        dc * JBW + (jci + 1) * 128],
                                    ufm[:, dc * S:(dc + 1) * S],
                                    start=(dc == 0), stop=(dc == NDC - 1)),
                                deps=mmdeps if dc == 0 else [],
                                sem=None if dc < NDC - 1 else "auto")
                        last_mm = lm
                        sc.wset(lm, breg)
                        if jci == NJC_IN - 1:
                            ring_cons[fi] = lm
                        if half == 0:
                            sg = sc.add("scalar",
                                        lambda e, pb=pb, ssel=ssel:
                                        e.activation(
                                            sig[:, ssel * S:(ssel + 1) * S],
                                            pb, AF.Sigmoid),
                                        deps=[lm] + sc.wdeps(f"sig{ssel}"))
                            sc.note_read(sg, breg)
                            sc.wset(sg, f"sig{ssel}")
                            sig_ops[jc] = sg
                        else:
                            hm = sc.add("vector",
                                        lambda e, pb=pb, jc=jc, ssel=ssel:
                                        e.tensor_tensor(
                                            hid[:, jc * S:(jc + 1) * S],
                                            sig[:, ssel * S:(ssel + 1) * S],
                                            pb, ALU.mult),
                                        deps=[sig_ops[jc], lm]
                                        + (sc.wdeps("hid") if jc == 0 else []))
                            sc.note_read(hm, breg, f"sig{ssel}")
                            last_hm = hm
            sc.note_read(last_mm, "ufm_w")
            sc.wset(last_hm, "hid")

            for eb in range(NEB):
                stop_mm = None
                if pump:
                    if eb == 0:
                        tail_drain_until(tail_marks["q3_end"])
                    tail_pump(pump)
                jc0 = 0
                for gsz in pb_parts:
                    if pump:
                        tail_pump(pump)
                    fi, ldop = ring_load(
                        q3v[:, jc0:jc0 + gsz, eb * EBW:(eb + 1) * EBW],
                        gsz * EBW, EBW,
                        extra_deps=[scratch_done["q3s"]], queue="scalar")
                    wap = ring_ap(fi, gsz * EBW)
                    part_last = None
                    for jci in range(gsz):
                        jc = jc0 + jci
                        for tc in range(NTC):
                            first = (jc == 0)
                            lastg = (jc == NJC - 1)
                            deps = []
                            if first:
                                deps = sc.wdeps(f"pM{tc}")
                                if tc == 0:
                                    deps += [last_hm, ldop]
                            elif jci == 0 and tc == 0:
                                deps = [ldop]
                            is_part_last = (jci == gsz - 1 and tc == NTC - 1)
                            mm = sc.add(
                                "tensor",
                                lambda e, jc=jc, tc=tc, jci=jci, wap=wap,
                                first=first, lastg=lastg:
                                e.matmul(
                                    pM[:, tc * 512:tc * 512 + EBW],
                                    hid[:, jc * S + tc * 128:
                                        jc * S + (tc + 1) * 128],
                                    wap[:, jci * EBW:(jci + 1) * EBW],
                                    start=first, stop=lastg),
                                deps=deps,
                                sem="auto" if is_part_last else None)
                            if is_part_last:
                                part_last = mm
                            if lastg and tc == NTC - 1:
                                stop_mm = mm
                    ring_cons[fi] = part_last
                    jc0 += gsz
                sc.note_read(stop_mm, "hid")
                if eb == 0 and ln_hook is not None:
                    ln_hook()
                    ln_hook = None
                for tc in range(NTC):
                    dst = xtile[:, tc * D + eb * EBW:tc * D + (eb + 1) * EBW]
                    ev = sc.add("vector",
                                lambda e, tc=tc, dst=dst:
                                e.scalar_tensor_tensor(
                                    dst, pM[:, tc * 512:tc * 512 + EBW], 0.5,
                                    dst, ALU.mult, ALU.add),
                                deps=[stop_mm] + sc.wdeps(xreg[tc]))
                    sc.note_read(ev, f"pM{tc}")
                    sc.wset(ev, xreg[tc])
                    if collect:
                        gdst = ubuf4[:, tc * D + eb * EBW:
                                     tc * D + (eb + 1) * EBW]
                        s2p = stats2[:, tc * 8 + 4 + eb:tc * 8 + 5 + eb]
                        s1p = stats2[:, tc * 8 + eb:tc * 8 + 1 + eb]
                        psq = sc.add("vector",
                                     lambda e, dst=dst, gdst=gdst, s2p=s2p:
                                     e.scalar_tensor_tensor(
                                         gdst, dst, 1.0, dst, ALU.mult,
                                         ALU.mult, accum_out=s2p),
                                     deps=[ev] + sc.wdeps(f"ub{tc}"))
                        sc.note_read(psq, xreg[tc])
                        sc.wset(psq, f"ub{tc}")
                        prd = sc.add("vector",
                                     lambda e, dst=dst, s1p=s1p:
                                     e.tensor_reduce(s1p, dst, AX, ALU.add),
                                     deps=[ev, psq])
                        sc.note_read(prd, xreg[tc])
                        stat_parts[tc].append(psq)
                        stat_parts[tc].append(prd)
            return

        def attn_ln_chain(tc, xtile, xreg, h_stop):
            asc = asc_slots[tc % 2]
            areg = f"asc{tc % 2}"
            hev = sc.add("scalar", lambda e, asc=asc: e.copy(asc[:], pM[:, 0:D]),
                         deps=[h_stop] + sc.wdeps(areg)
                         + (sc.wdeps("hid") if tc < 2 else []))
            if tc < 2:
                sc.note_read(hev, "hid")
            for b in range(NEB):
                sc.note_read(hev, f"pM{b}")
            sc.wset(hev, areg)
            slot = tc % NSLOT
            c0 = slot * 8
            s1 = stats[:, c0 + 0:c0 + 1]
            s2 = stats[:, c0 + 1:c0 + 2]
            nmu = stats[:, c0 + 2:c0 + 3]
            mu2 = stats[:, c0 + 3:c0 + 4]
            var = stats[:, c0 + 4:c0 + 5]
            stdv = stats[:, c0 + 5:c0 + 6]
            rstd = stats[:, c0 + 6:c0 + 7]
            sreg = f"stats{slot}"
            sgw = min(D, NJC_IN * S)
            sq = sc.add("vector", lambda e, sgw=sgw, asc=asc:
                e.scalar_tensor_tensor(
                    sig[:, 0:sgw], asc[:, 0:sgw], 1.0, asc[:, 0:sgw],
                    ALU.mult, ALU.mult, accum_out=s2),
                deps=[hev] + sc.wdeps(*[f"sig{i}" for i in range(NJC_IN)])
                + sc.wdeps(sreg))
            sc.note_read(sq, areg)
            assert D <= sgw, "attn square garbage dest too small"
            red = sc.add("vector", lambda e, asc=asc: e.tensor_reduce(
                s1, asc[:], AX, ALU.add), deps=[hev, sq])
            sc.note_read(red, areg)
            nm = sc.add("vector", lambda e: e.tensor_single_scalar(
                nmu, s1, -1.0 / D, ALU.mult), deps=[red])
            m2 = sc.add("vector", lambda e: e.tensor_mul(mu2, nmu, nmu),
                        deps=[nm])
            vv = sc.add("vector", lambda e: e.scalar_tensor_tensor(
                var, s2, 1.0 / D, mu2, ALU.mult, ALU.subtract),
                deps=[m2, sq])
            sd = sc.add("scalar", lambda e: e.activation(
                stdv, var, AF.Sqrt, bias=cEps[:, 0:1]),
                deps=[vv] + sc.rdeps("cEps"))
            rs = sc.add("vector", lambda e: e.reciprocal(rstd, stdv),
                        deps=[sd])
            for i in range(NJC_IN):
                sc.wset(sq, f"sig{i}")
            t1 = sc.add("vector", lambda e, asc=asc: e.scalar_tensor_tensor(
                asc[:], asc[:], nmu, Gbc[:], ALU.add, ALU.mult),
                deps=[nm, sq, red] + sc.rdeps("Gbc") + sc.wdeps(areg))
            sc.note_read(t1, "Gbc")
            sc.wset(t1, areg)
            t2 = sc.add("vector", lambda e, asc=asc: e.scalar_tensor_tensor(
                asc[:], asc[:], rstd, Bbc[:], ALU.mult, ALU.add),
                deps=[t1, rs] + sc.rdeps("Bbc"))
            sc.note_read(t2, "Bbc", sreg)
            sc.wset(t2, areg)
            sc.wset(rs, sreg)
            xa = xtile[:, tc * D:(tc + 1) * D]
            ra = sc.add("gpsimd", lambda e, xa=xa, asc=asc:
                        e.tensor_add(xa, xa, asc[:]),
                        deps=[t2] + sc.wdeps(xreg[tc]))
            sc.note_read(ra, areg)
            sc.wset(ra, xreg[tc])
            return ra

        def attn_sublayer(xtile, xreg, ufm_dep, reset_carry,
                          ln3_applies=None):
            if reset_carry:
                cr0 = sc.add("gpsimd", lambda e: e.memset(carry[:], 0.0),
                             deps=sc.wdeps("carry"))
                sc.wset(cr0, "carry")
                cr1 = sc.add("gpsimd", lambda e: e.memset(carry_bf[:], 0.0),
                             deps=sc.wdeps("carry_bf"))
                sc.wset(cr1, "carry_bf")

            last_zev = None
            last_stop = None
            for eq in range(NEB):
                fi, ldop = ring_load(
                    wtv[:, :, eq * EBW:(eq + 1) * EBW], NDC * EBW, EBW,
                    extra_deps=[scratch_done["wts"]])
                wap = ring_ap(fi, NDC * EBW)
                stop_mms = [None] * NTC
                for dc in range(NDC):
                    for tc in range(NTC):
                        deps = []
                        if dc == 0:
                            deps = sc.wdeps(f"pM{tc}")
                            if tc == 0:
                                deps += [ldop, ufm_dep]
                        mm = sc.add(
                            "tensor",
                            lambda e, dc=dc, tc=tc, wap=wap: e.matmul(
                                pM[:, tc * 512:tc * 512 + EBW],
                                ufm[:, dc * S + tc * 128:
                                    dc * S + (tc + 1) * 128],
                                wap[:, dc * EBW:(dc + 1) * EBW],
                                start=(dc == 0), stop=(dc == NDC - 1)),
                            deps=deps,
                            sem=None if dc < NDC - 1 else "auto")
                        if dc == NDC - 1:
                            stop_mms[tc] = mm
                ring_cons[fi] = stop_mms[NTC - 1]
                last_stop = stop_mms[NTC - 1]
                for tc in range(NTC):
                    zev = sc.add("scalar", lambda e, tc=tc, eq=eq:
                                 e.copy(
                                     z_sb(tc)[:, eq * EBW:(eq + 1) * EBW],
                                     pM[:, tc * 512:tc * 512 + EBW]),
                                 deps=[stop_mms[tc]]
                                 + (sc.wdeps("hid") if eq == 0 and tc == 0
                                    else []))
                    sc.note_read(zev, f"pM{tc}")
                    last_zev = zev
            sc.note_read(last_stop, "ufm_w")
            sc.wset(last_zev, "hid")
            z_dep = last_zev

            last_z_reader = None
            for tc in range(NTC):
                for eb in range(NEB):
                    fd = sc.wdeps(f"pM{eb}")
                    if eb == 0:
                        fd += [z_dep] + sc.rdeps("LtT")
                    sc.add("tensor", lambda e, eb=eb, tc=tc: e.matmul(
                        pM[:, eb * 512:eb * 512 + EBW], LtT[:],
                        z_sb(tc)[:, eb * EBW:(eb + 1) * EBW],
                        start=True, stop=False),
                        deps=fd, sem=None)
                cp_deps = sc.rdeps("CPb", "carry_bf")
                h_stop = None
                for eb in range(NEB):
                    h_stop = sc.add("tensor", lambda e, eb=eb: e.matmul(
                        pM[:, eb * 512:eb * 512 + EBW], CPb[:],
                        carry_bf[:, eb * EBW:(eb + 1) * EBW],
                        start=False, stop=True),
                        deps=cp_deps if eb == 0 else [],
                        sem=None if eb < NEB - 1 else "auto")
                sc.note_read(h_stop, "carry_bf", "LtT", "CPb")
                for b in range(NEB):
                    sc.wset(h_stop, f"pM{b}")

                cu_prev = None
                for eb in range(NEB):
                    mmc = sc.add("tensor", lambda e, eb=eb, tc=tc: e.matmul(
                        pA[0:K, 0:EBW], CUb[:],
                        z_sb(tc)[:, eb * EBW:(eb + 1) * EBW],
                        start=True, stop=True),
                        deps=[z_dep] + sc.rdeps("CUb") + sc.wdeps("pA0")
                        + ([cu_prev] if cu_prev else []))
                    sc.wset(mmc, "pA0")
                    sc.note_read(mmc, "CUb")
                    cup = sc.add("vector",
                                 lambda e, eb=eb: e.scalar_tensor_tensor(
                                     carry[:, eb * EBW:(eb + 1) * EBW],
                                     carry[:, eb * EBW:(eb + 1) * EBW],
                                     lam128[:, 0:1], pA[0:K, 0:EBW],
                                     ALU.mult, ALU.add),
                                 deps=[mmc] + sc.wdeps("carry")
                                 + sc.rdeps("lam128"))
                    sc.note_read(cup, "pA0", "lam128")
                    sc.wset(cup, "carry")
                    cu_prev = cup
                last_z_reader = mmc
                cbf = sc.add("gpsimd", lambda e: e.tensor_copy(
                    carry_bf[:], carry[:]),
                    deps=[cu_prev] + sc.wdeps("carry_bf"))
                sc.note_read(cbf, "carry")
                sc.wset(cbf, "carry_bf")

                attn_ln_chain(tc, xtile, xreg, h_stop)
                if ln3_applies is not None:
                    ln3_applies.append(emit_ln(
                        xtile[:, tc * D:(tc + 1) * D], tc % NSLOT,
                        xreg[tc], tc))
            sc.note_read(last_z_reader, "hid")

        xb1_addr = nc.lookup_mls("xb1").memorylocations[0].addr
        assert NTC * D * 4 >= 14 * D
        t_wb0 = nc.alloc_sbuf_tensor_at("t_wb0", [128, D], F32,
                                        offset=xb1_addr)
        t_wb1 = nc.alloc_sbuf_tensor_at("t_wb1", [128, D], F32,
                                        offset=xb1_addr + 4 * D)
        t_qa = nc.alloc_sbuf_tensor_at("t_qa", [128, D], F32,
                                       offset=xb1_addr + 8 * D)
        t_qc = nc.alloc_sbuf_tensor_at("t_qc", [128, D], BF16,
                                       offset=xb1_addr + 12 * D)
        RACCT = 40
        GT1, GT2 = 41, 42
        tail_done = []

        tail_thunks = []

        def build_tail_thunks():
            wbufs = [t_wb0, t_wb1]
            state = {"bf": [list(pro_done), list(pro_done)], "last_q": None,
                     "gamma_ready": None, "prev": None}
            tmats = [
                ("q3s", w3t_in, H, D, q3s, True, 2, "sg2"),
                ("wts", Wt_in, D, D, wts, False, -1, "sg3"),
            ]
            for (mname, w_ap, NR, NCOL, scratch, doq, rcol, sgkey) in tmats:
                nch = NR // 128
                if doq:
                    def t_z0():
                        z0 = sc.add("vector", lambda e: e.memset(
                            stats[:, RACCT:RACCT + 1], 0.0),
                            deps=list(pro_done))
                        state["prev"] = z0
                    tail_thunks.append(t_z0)
                    for c in range(nch):
                        def t_p1(c=c, w_ap=w_ap, NCOL=NCOL):
                            buf = wbufs[c % 2]
                            ld = sc.dma("sync", buf[:, 0:NCOL],
                                        w_ap[c * 128:(c + 1) * 128, :],
                                        deps=state["bf"][c % 2],
                                        key=f"tl{c % 2}")
                            red = sc.add("vector", lambda e, b=buf, n=NCOL:
                                         e.tensor_reduce(
                                             stats[:, GT1:GT1 + 1],
                                             b[:, 0:n], AX, ALU.add,
                                             apply_absolute_value=True),
                                         deps=[ld, state["prev"]])
                            state["prev"] = sc.add(
                                "vector", lambda e: e.tensor_add(
                                    stats[:, RACCT:RACCT + 1],
                                    stats[:, RACCT:RACCT + 1],
                                    stats[:, GT1:GT1 + 1]), deps=[red])
                            state["bf"][c % 2] = [red]
                        tail_thunks.append(t_p1)

                    def t_gam(NR=NR, NCOL=NCOL, rcol=rcol):
                        tg = sc.add("tensor", lambda e: e.matmul(
                            pA[0:1, 300:301], stats[:, RACCT:RACCT + 1],
                            onescolf[:], start=True, stop=True),
                            deps=[state["prev"]] + sc.wdeps("pA0"))
                        sc.wset(tg, "pA0")
                        gm = sc.add("vector", lambda e, n=NR * NCOL:
                                    e.tensor_scalar(
                                        stats[0:1, GT1:GT1 + 1],
                                        pA[0:1, 300:301], 1.0 / n, Q_EPS,
                                        ALU.mult, ALU.add), deps=[tg])
                        sc.note_read(gm, "pA0")
                        rc = sc.add("vector", lambda e: e.reciprocal(
                            stats[0:1, GT2:GT2 + 1],
                            stats[0:1, GT1:GT1 + 1]), deps=[gm])
                        tb = sc.add("tensor", lambda e: e.matmul(
                            pA[:, 301:302], ones1f[:],
                            stats[0:1, GT2:GT2 + 1],
                            start=True, stop=True),
                            deps=[rc] + sc.wdeps("pA0"))
                        sc.wset(tb, "pA0")
                        gr = sc.add("vector", lambda e, rcol=rcol:
                                    e.tensor_copy(
                                        recip[:, rcol:rcol + 1],
                                        pA[:, 301:302]), deps=[tb])
                        sc.note_read(gr, "pA0")
                        state["gamma_ready"] = gr
                    tail_thunks.append(t_gam)

                for c in range(nch):
                    def t_p2(c=c, w_ap=w_ap, NCOL=NCOL, doq=doq, rcol=rcol,
                             scratch=scratch, sgkey=sgkey, mname=mname,
                             nch=nch):
                        buf = wbufs[c % 2]
                        ld = sc.dma("sync", buf[:, 0:NCOL],
                                    w_ap[c * 128:(c + 1) * 128, :],
                                    deps=state["bf"][c % 2],
                                    key=f"tl{c % 2}")
                        if doq:
                            qa_ = sc.add("vector",
                                         lambda e, b=buf, n=NCOL, rcol=rcol:
                                         e.tensor_scalar(
                                             t_qa[:, 0:n], b[:, 0:n],
                                             recip[:, rcol:rcol + 1], 0.5,
                                             ALU.mult, ALU.is_ge),
                                         deps=[ld, state["gamma_ready"],
                                               state["last_q"]])
                            qb_ = sc.add("vector",
                                         lambda e, b=buf, n=NCOL, rcol=rcol:
                                         e.tensor_scalar(
                                             t_qc[:, 0:n], b[:, 0:n],
                                             recip[:, rcol:rcol + 1], -0.5,
                                             ALU.mult, ALU.is_le),
                                         deps=[ld, state["gamma_ready"],
                                               state["last_q"]])
                            q_ = sc.add("vector", lambda e, n=NCOL:
                                        e.tensor_tensor(
                                            t_qc[:, 0:n], t_qa[:, 0:n],
                                            t_qc[:, 0:n], ALU.subtract),
                                        deps=[qa_, qb_])
                        else:
                            q_ = sc.add("vector", lambda e, b=buf, n=NCOL:
                                        e.tensor_copy(t_qc[:, 0:n],
                                                      b[:, 0:n]),
                                        deps=[ld, state["last_q"]])
                        state["bf"][c % 2] = [q_]
                        st_op = sc.dma("sync",
                                       scratch[c * 128:(c + 1) * 128, :],
                                       t_qc[:, 0:NCOL],
                                       deps=[q_], key=sgkey)
                        state["last_q"] = st_op
                        scratch_done[mname] = st_op
                        if mname == "wts" and c == nch - 1:
                            tail_done.append(st_op)
                    tail_thunks.append(t_p2)
                if mname == "q3s":
                    tail_marks["q3_end"] = len(tail_thunks)

        tail_marks = {}
        build_tail_thunks()
        tail_state = {"emitted": 0}

        def tail_pump(n):
            k = 0
            while tail_thunks and k < n:
                tail_thunks.pop(0)()
                tail_state["emitted"] += 1
                k += 1

        def tail_drain_until(n):
            while tail_thunks and tail_state["emitted"] < n:
                tail_thunks.pop(0)()
                tail_state["emitted"] += 1

        def tail_drain():
            while tail_thunks:
                tail_thunks.pop(0)()
                tail_state["emitted"] += 1

        last_store = None
        y_stores = [None, None]
        x_dep = [None, None]
        pend_ln = [None]

        x_dep[0] = x0_ld
        pend_ln[0] = emit_ln_stats_all(xb[0],
                                       [f"xb0_{t}" for t in range(NTC)],
                                       extra_dep=x0_ld)
        ev0 = emit_transposes_all(pend_ln[0])

        for g in range(NTILE):
            si, ti = divmod(g, NT_SEQ)
            p = g % 2
            t0 = ti * S
            xtile = xb[p]
            xreg = [f"xb{p}_{tc}" for tc in range(NTC)]
            def emit_next_load(g=g):
                si2, ti2 = divmod(g + 1, NT_SEQ)
                pn = (g + 1) % 2
                ld = sc.dma("scalar",
                            xb[pn][:].rearrange("p (tc d) -> p tc d", d=D),
                            x_in[si2, ti2 * S:ti2 * S + S, :].rearrange(
                                "(tc q) d -> q tc d", q=128),
                            deps=sc.wdeps(
                                *[f"xb{pn}_{t}" for t in range(NTC)])
                            + (tail_done if g == 0 else []),
                            key=f"xl{pn}")
                x_dep[pn] = ld
                sc.wset(ld, *[f"xb{pn}_{t}" for t in range(NTC)])

            if g + 1 < NTILE and g != 0:
                emit_next_load()

            if g == 0:
                ev = ev0
            else:
                ev = emit_transposes_all(pend_ln[0])
            stat_parts = [[] for _ in range(NTC)]
            ffn_sublayer(xtile, xreg, ev, collect=True, stat_parts=stat_parts,
                         pump=5 if g == 0 else 0)
            applies = [emit_ln_fin(xtile, xreg, tc, stat_parts[tc])
                       for tc in range(NTC)]
            ev = emit_transposes_all(applies)
            ln3_applies = []
            if g == 0:
                tail_drain()
                if NTILE > 1:
                    emit_next_load()
            attn_sublayer(xtile, xreg, ev, reset_carry=(ti == 0 and g > 0),
                          ln3_applies=ln3_applies)
            ev = emit_transposes_all(ln3_applies)
            if g + 1 < NTILE:
                pn2 = (g + 1) % 2
                xreg_n = [f"xb{pn2}_{t}" for t in range(NTC)]

                def ln_hook(pn2=pn2, xreg_n=xreg_n):
                    pend_ln[0] = emit_ln_stats_all(xb[pn2], xreg_n,
                                                   extra_dep=x_dep[pn2])
                ffn_sublayer(xtile, xreg, ev, ln_hook=ln_hook)
            else:
                ffn_sublayer(xtile, xreg, ev)

            st = sc.dma("scalar",
                        y_out[si, t0:t0 + S, :].rearrange(
                            "(tc q) d -> q tc d", q=128),
                        xtile[:].rearrange("p (tc d) -> p tc d", d=D),
                        deps=sc.rdeps(*xreg), key=f"ys{p}")
            sc.note_read(st, *xreg)
            y_stores[p] = st
            last_store = st

        sc.wait_only("scalar", [s for s in y_stores if s is not None])

        sc.finalize()
        with nc.Block() as block:
            sc.emit_all(block)

    return {e: sum(1 for o in sc.ops if o.engine == e) for e in ENGINES}


def kernel(**kw):
    x = np.asarray(kw["L0_attn_x"], dtype=np.float32)
    W = np.ascontiguousarray(np.asarray(kw["L0_attn_W"], dtype=np.float32))
    lam_logit = np.ascontiguousarray(
        np.asarray(kw["L0_attn_lam_logit"], dtype=np.float32))
    eta = np.ascontiguousarray(np.asarray(kw["L0_attn_eta"], dtype=np.float32))
    gamma = np.ascontiguousarray(
        np.asarray(kw["L0_attn_gamma"], dtype=np.float32))
    beta = np.ascontiguousarray(
        np.asarray(kw["L0_attn_beta"], dtype=np.float32))
    w1t = np.ascontiguousarray(np.asarray(kw["L0_ffn_w1"], dtype=np.float32).T)
    w2t = np.ascontiguousarray(np.asarray(kw["L0_ffn_w2"], dtype=np.float32).T)
    w3t = np.ascontiguousarray(np.asarray(kw["L0_ffn_w3"], dtype=np.float32).T)
    Wt = np.ascontiguousarray(W.T)

    B = x.shape[0]
    n_cores = 8
    bl = B // n_cores
    cfg = dict(CFG_FULL)
    cfg.update(B_loc=bl, T=x.shape[1], D=x.shape[2], H=w1t.shape[1])

    nc = bass.Bass("TRN2", target_bir_lowering=False)
    build_program(nc, cfg)
    nc.finalize()

    in_maps = []
    for c in range(n_cores):
        in_maps.append({
            "x": np.ascontiguousarray(x[c * bl:(c + 1) * bl]),
            "Wt": Wt, "lam_logit": lam_logit, "eta": eta,
            "gamma": gamma, "beta": beta,
            "w1t": w1t, "w2t": w2t, "w3t": w3t,
        })
    res = run_bass_kernel_spmd(nc, in_maps, core_ids=list(range(n_cores)))
    return np.concatenate([np.asarray(r["y"]) for r in res.results], axis=0)


# version 16
# speedup vs baseline: 1.0227x; 1.0227x over previous
import numpy as np
import ml_dtypes
from contextlib import ExitStack

import concourse.bass as bass
import concourse.mybir as mybir
from concourse.bass_utils import run_bass_kernel_spmd

F32 = mybir.dt.float32
BF16 = mybir.dt.bfloat16
ALU = mybir.AluOpType
AF = mybir.ActivationFunctionType
AX = mybir.AxisListType.X

LN_EPS = 1e-5
Q_EPS = 1e-8
TC = 128

ENGINES = ("sync", "scalar", "vector", "tensor", "gpsimd")

CFG_FULL = dict(B_loc=2, T=2048, D=2048, H=5632, S=512)


class Op:
    __slots__ = ("engine", "emit", "deps", "sem", "inc_n", "val")

    def __init__(self, engine, emit, deps, sem, inc_n):
        self.engine = engine
        self.emit = emit
        self.deps = deps
        self.sem = sem
        self.inc_n = inc_n
        self.val = None


class Reg:
    __slots__ = ("w", "readers")

    def __init__(self):
        self.w = None
        self.readers = []


class Sched:
    def __init__(self, sems, dma_sems):
        self.sems = sems
        self.dma_sems = dma_sems
        self.ops = []
        self.regs = {}

    def reg(self, name):
        if name not in self.regs:
            self.regs[name] = Reg()
        return self.regs[name]

    def rdeps(self, *names):
        d = []
        for n in names:
            r = self.reg(n)
            if r.w is not None:
                d.append(r.w)
        return d

    def wdeps(self, *names):
        d = []
        for n in names:
            r = self.reg(n)
            d.extend(r.readers)
            if r.w is not None:
                d.append(r.w)
        return d

    def note_read(self, op, *names):
        for n in names:
            self.reg(n).readers.append(op)

    def wset(self, op, *names):
        for n in names:
            r = self.reg(n)
            r.w = op
            r.readers = []

    def add(self, engine, emit, deps=(), sem="auto"):
        if sem == "auto":
            s, n = self.sems[engine], 1
        elif sem is None:
            s, n = None, 0
        else:
            s, n = sem
        deps = [d for d in deps if d is not None]
        op = Op(engine, emit, deps, s, n)
        self.ops.append(op)
        return op

    def dma(self, queue, dst, src, deps=(), key=None):
        s = self.dma_sems[key if key is not None else queue]
        return self.add(queue, lambda e, d=dst, s_=src: e.dma_start(d, s_),
                        deps=deps, sem=(s, 16))

    def wait_only(self, engine, deps):
        return self.add(engine, lambda e: None, deps=deps, sem=None)

    def finalize(self):
        counts = {}
        for op in self.ops:
            if op.sem is not None:
                k = id(op.sem)
                counts[k] = counts.get(k, 0) + op.inc_n
                op.val = counts[k]

    def emit_all(self, block):
        for eng_name in ENGINES:
            ops = [o for o in self.ops if o.engine == eng_name]
            if not ops:
                continue

            def body(eng, ops=ops):
                waited = {}
                for op in ops:
                    for d in op.deps:
                        assert d.sem is not None, "dep on op without sem"
                        k = id(d.sem)
                        if waited.get(k, -1) >= d.val:
                            continue
                        eng.wait_ge(d.sem, d.val)
                        waited[k] = d.val
                    inst = op.emit(eng)
                    if op.sem is not None:
                        inst.then_inc(op.sem, op.inc_n)

            getattr(block, eng_name)(body)


def build_program(nc: bass.Bass, cfg):
    B_loc, T, D, H, S = (cfg[k] for k in ("B_loc", "T", "D", "H", "S"))
    NTC = S // TC
    NDC = D // 128
    NJC = H // 128
    JBW = min(H, 512)
    NJB = H // JBW
    NJC_IN = JBW // 128
    EBW = min(D, 512)
    NEB = D // EBW
    NT_SEQ = T // S
    NTILE = B_loc * NT_SEQ
    K = 4
    assert S % TC == 0 and D % 128 == 0 and H % JBW == 0 and D % EBW == 0

    max_g = max(1, 8192 // EBW)
    pb_parts = []
    r = NJC
    while r > 0:
        g = min(max_g, r)
        pb_parts.append(g)
        r -= g

    x_in = nc.dram_tensor("x", [B_loc, T, D], F32, kind="ExternalInput").ap()
    Wt_in = nc.dram_tensor("Wt", [D, D], F32, kind="ExternalInput").ap()
    lam_in = nc.dram_tensor("lam_logit", [K], F32, kind="ExternalInput").ap()
    eta_in = nc.dram_tensor("eta", [K], F32, kind="ExternalInput").ap()
    gamma_in = nc.dram_tensor("gamma", [D], F32, kind="ExternalInput").ap()
    beta_in = nc.dram_tensor("beta", [D], F32, kind="ExternalInput").ap()
    w1t_in = nc.dram_tensor("w1t", [D, H], F32, kind="ExternalInput").ap()
    w2t_in = nc.dram_tensor("w2t", [D, H], F32, kind="ExternalInput").ap()
    w3t_in = nc.dram_tensor("w3t", [H, D], F32, kind="ExternalInput").ap()
    y_out = nc.dram_tensor("y", [B_loc, T, D], F32, kind="ExternalOutput").ap()

    q1s = nc.dram_tensor("q1s", [D, H], BF16, kind="Internal").ap()
    q2s = nc.dram_tensor("q2s", [D, H], BF16, kind="Internal").ap()
    q3s = nc.dram_tensor("q3s", [H, D], BF16, kind="Internal").ap()
    wts = nc.dram_tensor("wts", [D, D], BF16, kind="Internal").ap()

    with ExitStack() as ctx:
        def sbuf(name, shape, dt=F32):
            return ctx.enter_context(nc.sbuf_tensor(name, shape, dt))

        def psum_t(name, shape, dt=F32):
            return ctx.enter_context(nc.psum_tensor(name, shape, dt))

        def sem(name):
            return ctx.enter_context(nc.semaphore(name))

        sems = {e: sem("s_" + e) for e in ENGINES}
        dma_keys = ("par", "sg0", "sg1", "sg2", "sg3", "wl0", "wl1",
                    "r0", "r1", "xl0", "xl1", "ys0", "ys1", "tl0", "tl1")
        dma_sems = {k: sem("sd_" + k) for k in dma_keys}
        sc = Sched(sems, dma_sems)

        pT = psum_t("pT", [128, 1024], BF16)

        Gbc = sbuf("Gbc", [128, D])
        Bbc = sbuf("Bbc", [128, D])
        carry = sbuf("carry", [K, D])
        carry_bf = sbuf("carry_bf", [K, D], BF16)
        LtT = sbuf("LtT", [TC, TC], BF16)
        CPb = sbuf("CPb", [K, TC], BF16)
        CUb = sbuf("CUb", [TC, K], BF16)
        lam128 = sbuf("lam128", [K, 1])
        ident = sbuf("ident", [128, 128], BF16)
        ones1f = sbuf("ones1f", [1, 128])
        onescolf = sbuf("onescolf", [128, 1])
        stats = sbuf("stats", [128, 48])
        stats2 = sbuf("stats2", [128, 32])
        recip = sbuf("recip", [128, 3])
        cEps = sbuf("cEps", [128, 1])
        cTC = sbuf("cTC", [128, 1])

        scratch_done = {}

        xb0_t = sbuf("xb0", [128, (S // TC) * D])
        ubuf4 = sbuf("ubuf4", [128, (S // TC) * D], BF16)
        ufm = sbuf("ufm", [128, (D // 128) * S], BF16)

        MX = max(D, H)
        with nc.sbuf_tensor("pro_w0", [128, MX], F32) as pro_w0, \
             nc.sbuf_tensor("pro_w1", [128, MX], F32) as pro_w1, \
             nc.sbuf_tensor("pro_qa", [128, MX], F32) as pro_qa, \
             nc.sbuf_tensor("pro_qb", [128, MX], F32) as pro_qb, \
             nc.sbuf_tensor("pro_qc", [128, MX], BF16) as pro_qc, \
             nc.sbuf_tensor("pro_m", [128, 1024], F32) as pro_m, \
             nc.psum_tensor("pPro", [128, 2048], F32) as pPro:

            LAMR, ETAR, LNLR = 0, 4, 8
            LAMC, LNLC, ETAC = 12, 13, 14
            LNL128, ETA128 = 16, 20
            ICOL, ICOLR, RACC, GTMP, GTMP2, CUS = 24, 25, 26, 27, 28, 29
            M0, TS, MSK, ACC, EK, TK, IROW = 32, 160, 288, 416, 544, 672, 800

            sc.dma("scalar", pro_m[0:1, LAMR:LAMR + K], lam_in[None, :],
                   key="par")
            sc.dma("scalar", pro_m[0:1, ETAR:ETAR + K], eta_in[None, :],
                   key="par")
            sc.dma("scalar", pro_m[0:K, LAMC:LAMC + 1], lam_in[:, None],
                   key="par")
            sc.dma("scalar", pro_m[0:K, ETAC:ETAC + 1], eta_in[:, None],
                   key="par")
            sc.dma("scalar", pro_w0[0:1, 0:D], gamma_in[None, :], key="par")
            d_params = sc.dma("scalar", pro_w1[0:1, 0:D], beta_in[None, :],
                              key="par")
            d_lam_r = d_eta_r = d_lam_c = d_eta_c = d_gam = d_bet = d_params

            g_ic = sc.add("gpsimd", lambda e: e.iota(
                pro_m[:, ICOL:ICOL + 1], [[0, 1]], channel_multiplier=1,
                allow_small_or_imprecise_dtypes=True))
            g_icr = sc.add("gpsimd", lambda e: e.iota(
                pro_m[:, ICOLR:ICOLR + 1], [[0, 1]], base=TC - 1,
                channel_multiplier=-1, allow_small_or_imprecise_dtypes=True))
            g_ir = sc.add("gpsimd", lambda e: e.iota(
                pro_m[0:1, IROW:IROW + 128], [[1, 128]], channel_multiplier=0,
                allow_small_or_imprecise_dtypes=True))
            g_o1 = sc.add("gpsimd", lambda e: e.memset(ones1f[:], 1.0))
            g_oc = sc.add("gpsimd", lambda e: e.memset(onescolf[:], 1.0))
            g_c0 = sc.add("gpsimd", lambda e: e.memset(carry[:], 0.0))
            g_c1 = sc.add("gpsimd", lambda e: e.memset(carry_bf[:], 0.0))
            g_ce = sc.add("gpsimd", lambda e: e.memset(cEps[:], LN_EPS))
            g_ct = sc.add("gpsimd", lambda e: e.memset(cTC[:], float(TC)))
            sc.wset(g_ce, "cEps")
            sc.wset(g_ct, "cTC")
            sc.wset(g_c0, "carry")
            sc.wset(g_c1, "carry_bf")

            a_sr = sc.add("scalar", lambda e: e.activation(
                pro_m[0:1, LAMR:LAMR + K], pro_m[0:1, LAMR:LAMR + K],
                AF.Sigmoid), deps=[d_lam_r])
            a_sc = sc.add("scalar", lambda e: e.activation(
                pro_m[0:K, LAMC:LAMC + 1], pro_m[0:K, LAMC:LAMC + 1],
                AF.Sigmoid), deps=[d_lam_c])
            a_lr = sc.add("scalar", lambda e: e.activation(
                pro_m[0:1, LNLR:LNLR + K], pro_m[0:1, LAMR:LAMR + K], AF.Ln),
                deps=[a_sr])
            a_lc = sc.add("scalar", lambda e: e.activation(
                pro_m[0:K, LNLC:LNLC + 1], pro_m[0:K, LAMC:LAMC + 1], AF.Ln),
                deps=[a_sc])
            a_l128 = sc.add("scalar", lambda e: e.activation(
                lam128[:, 0:1], pro_m[0:K, LNLC:LNLC + 1], AF.Exp,
                scale=cTC[0:K, 0:1]), deps=[a_lc, g_ct])
            sc.wset(a_l128, "lam128")

            def bcast(dst_ap, src_row_ap, width, deps):
                t = None
                first_deps = list(deps) + [g_o1] + sc.wdeps("pPro")
                for c0 in range(0, width, 512):
                    cw = min(512, width - c0)
                    t = sc.add("tensor",
                               lambda e, s=src_row_ap, c0=c0, cw=cw: e.matmul(
                                   pPro[:, c0:c0 + cw], ones1f[:],
                                   s[:, c0:c0 + cw], start=True, stop=True),
                               deps=first_deps if c0 == 0 else [],
                               sem=None if c0 + cw < width else "auto")
                sc.wset(t, "pPro")
                v = sc.add("vector",
                           lambda e, d=dst_ap, w=width: e.tensor_copy(
                               d, pPro[:, 0:w]), deps=[t])
                sc.note_read(v, "pPro")
                return v

            v_m0 = bcast(pro_m[:, M0:M0 + 128], pro_m[0:1, IROW:IROW + 128],
                         128, [g_ir])
            v_l128 = bcast(pro_m[:, LNL128:LNL128 + K],
                           pro_m[0:1, LNLR:LNLR + K], K, [a_lr])
            v_e128 = bcast(pro_m[:, ETA128:ETA128 + K],
                           pro_m[0:1, ETAR:ETAR + K], K, [d_eta_r])
            v_gbc = bcast(Gbc[:], pro_w0[0:1, 0:D], D, [d_gam])
            sc.wset(v_gbc, "Gbc")
            v_bbc = bcast(Bbc[:], pro_w1[0:1, 0:D], D, [d_bet])
            sc.wset(v_bbc, "Bbc")

            v_ts = sc.add("vector", lambda e: e.tensor_scalar(
                pro_m[:, TS:TS + 128], pro_m[:, M0:M0 + 128],
                pro_m[:, ICOL:ICOL + 1], None, ALU.subtract),
                deps=[v_m0, g_ic])
            v_msk = sc.add("vector", lambda e: e.tensor_single_scalar(
                pro_m[:, MSK:MSK + 128], pro_m[:, TS:TS + 128], 0.0,
                ALU.is_ge), deps=[v_ts])
            v_eq = sc.add("vector", lambda e: e.tensor_single_scalar(
                pro_m[:, ACC:ACC + 128], pro_m[:, TS:TS + 128], 0.0,
                ALU.is_equal), deps=[v_msk])
            v_id = sc.add("vector", lambda e: e.tensor_copy(
                ident[:], pro_m[:, ACC:ACC + 128]), deps=[v_eq])
            sc.wset(v_id, "ident")
            v_tsc = sc.add("vector", lambda e: e.tensor_scalar_max(
                pro_m[:, TS:TS + 128], pro_m[:, TS:TS + 128], 0.0),
                deps=[v_eq, v_id])

            last = v_tsc
            for k in range(K):
                ek = sc.add("scalar", lambda e, k=k: e.activation(
                    pro_m[:, EK:EK + 128], pro_m[:, TS:TS + 128], AF.Exp,
                    scale=pro_m[:, LNL128 + k:LNL128 + k + 1]),
                    deps=[last, v_l128, v_tsc, a_l128])
                tk = sc.add("vector", lambda e, k=k: e.scalar_tensor_tensor(
                    pro_m[:, TK:TK + 128], pro_m[:, EK:EK + 128],
                    pro_m[:, ETA128 + k:ETA128 + k + 1],
                    pro_m[:, MSK:MSK + 128], ALU.mult, ALU.mult),
                    deps=[ek, v_e128, v_msk])
                last = sc.add("vector", lambda e: e.tensor_add(
                    pro_m[:, ACC:ACC + 128], pro_m[:, ACC:ACC + 128],
                    pro_m[:, TK:TK + 128]), deps=[tk])
            v_lt = sc.add("vector", lambda e: e.tensor_copy(
                LtT[:], pro_m[:, ACC:ACC + 128]), deps=[last])
            sc.wset(v_lt, "LtT")

            a_cp = sc.add("scalar", lambda e: e.activation(
                pro_m[0:K, EK:EK + TC], pro_m[0:K, M0:M0 + TC], AF.Exp,
                scale=pro_m[0:K, LNLC:LNLC + 1],
                bias=pro_m[0:K, LNLC:LNLC + 1]),
                deps=[a_lc, v_m0, last])
            v_cp1 = sc.add("vector", lambda e: e.tensor_scalar(
                pro_m[0:K, TK:TK + TC], pro_m[0:K, EK:EK + TC],
                pro_m[0:K, ETAC:ETAC + 1], None, ALU.mult),
                deps=[a_cp, d_eta_c, v_lt])
            v_cp = sc.add("vector", lambda e: e.tensor_copy(
                CPb[:], pro_m[0:K, TK:TK + TC]), deps=[v_cp1])
            sc.wset(v_cp, "CPb")

            a_cu = sc.add("scalar", lambda e: e.activation(
                pro_m[:, CUS:CUS + K], pro_m[:, LNL128:LNL128 + K], AF.Exp,
                scale=pro_m[:, ICOLR:ICOLR + 1]),
                deps=[v_l128, g_icr, a_cp])
            v_cu = sc.add("vector", lambda e: e.tensor_copy(
                CUb[:], pro_m[:, CUS:CUS + K]), deps=[a_cu, v_cp])
            sc.wset(v_cu, "CUb")

            x0_ld = sc.dma("scalar",
                           xb0_t[:].rearrange("p (tc d) -> p tc d", d=D),
                           x_in[0, 0:S, :].rearrange(
                               "(tc q) d -> q tc d", q=128),
                           deps=[], key="xl0")
            sc.wset(x0_ld, *[f"xb0_{t}" for t in range(S // TC)])

            mats = [
                ("q1s", w1t_in, D, H, q1s, True, 0, "sg0"),
                ("q2s", w2t_in, D, H, q2s, True, 1, "sg1"),
            ]
            wbufs = [pro_w0, pro_w1]
            buf_free = [[v_gbc], [v_bbc]]
            last_qc = None
            qc_free = v_cu

            for (mname, w_ap, NR, NCOL, scratch, doq, rcol, sgkey) in mats:
                nch = NR // 128
                gamma_ready = None
                if doq:
                    z0 = sc.add("gpsimd", lambda e: e.memset(
                        pro_m[:, RACC:RACC + 1], 0.0), deps=sc.wdeps("racc"))
                    sc.wset(z0, "racc")
                    prev = z0
                    for c in range(nch):
                        buf = wbufs[c % 2]
                        ld = sc.dma("sync", buf[:, 0:NCOL],
                                    w_ap[c * 128:(c + 1) * 128, :],
                                    deps=buf_free[c % 2], key=f"wl{c % 2}")
                        red = sc.add("vector", lambda e, b=buf, n=NCOL:
                                     e.tensor_reduce(
                                         pro_m[:, GTMP:GTMP + 1], b[:, 0:n],
                                         AX, ALU.add,
                                         apply_absolute_value=True),
                                     deps=[ld, prev])
                        prev = sc.add("vector", lambda e: e.tensor_add(
                            pro_m[:, RACC:RACC + 1], pro_m[:, RACC:RACC + 1],
                            pro_m[:, GTMP:GTMP + 1]), deps=[red])
                        buf_free[c % 2] = [red]
                    tg = sc.add("tensor", lambda e: e.matmul(
                        pPro[0:1, 300:301], pro_m[:, RACC:RACC + 1],
                        onescolf[:], start=True, stop=True),
                        deps=[prev, g_oc] + sc.wdeps("pPro"))
                    sc.wset(tg, "pPro")
                    sc.note_read(tg, "racc")
                    gm = sc.add("vector", lambda e, n=NR * NCOL:
                                e.tensor_scalar(
                                    pro_m[0:1, GTMP:GTMP + 1],
                                    pPro[0:1, 300:301], 1.0 / n, Q_EPS,
                                    ALU.mult, ALU.add), deps=[tg])
                    sc.note_read(gm, "pPro")
                    rc = sc.add("vector", lambda e: e.reciprocal(
                        pro_m[0:1, GTMP2:GTMP2 + 1],
                        pro_m[0:1, GTMP:GTMP + 1]), deps=[gm])
                    gamma_ready = bcast(recip[:, rcol:rcol + 1],
                                        pro_m[0:1, GTMP2:GTMP2 + 1], 1, [rc])

                for c in range(nch):
                    buf = wbufs[c % 2]
                    ld = sc.dma("sync", buf[:, 0:NCOL],
                                w_ap[c * 128:(c + 1) * 128, :],
                                deps=buf_free[c % 2], key=f"wl{c % 2}")
                    if doq:
                        qa = sc.add("vector",
                                    lambda e, b=buf, n=NCOL, rcol=rcol:
                                    e.tensor_scalar(
                                        pro_qa[:, 0:n], b[:, 0:n],
                                        recip[:, rcol:rcol + 1], 0.5,
                                        ALU.mult, ALU.is_ge),
                                    deps=[ld, gamma_ready, last_qc])
                        qb = sc.add("vector",
                                    lambda e, b=buf, n=NCOL, rcol=rcol:
                                    e.tensor_scalar(
                                        pro_qb[:, 0:n], b[:, 0:n],
                                        recip[:, rcol:rcol + 1], -0.5,
                                        ALU.mult, ALU.is_le),
                                    deps=[ld, gamma_ready, last_qc])
                        qc = sc.add("gpsimd", lambda e, n=NCOL: e.tensor_sub(
                            pro_qc[:, 0:n], pro_qa[:, 0:n], pro_qb[:, 0:n]),
                            deps=[qa, qb, qc_free])
                    else:
                        qc = sc.add("gpsimd", lambda e, b=buf, n=NCOL:
                                    e.tensor_copy(pro_qc[:, 0:n], b[:, 0:n]),
                                    deps=[ld, qc_free, last_qc])
                    last_qc = qc
                    buf_free[c % 2] = [qc]
                    st_op = sc.dma("scalar",
                                   scratch[c * 128:(c + 1) * 128, :],
                                   pro_qc[:, 0:NCOL],
                                   deps=[qc], key=sgkey)
                    qc_free = st_op
                    scratch_done[mname] = st_op

            pro_done = [scratch_done[m] for m in ("q1s", "q2s")]
            pro_done += [v_gbc, v_bbc, v_lt, v_cp, v_cu, a_l128, g_c0, g_c1,
                         qc_free, last_qc]


        xb = [xb0_t, sbuf("xb1", [128, NTC * D])]
        HIDW = max(NJC * S, NTC * D + 4 * D)
        hid = sbuf("hid", [128, HIDW], BF16)
        ring = sbuf("ring", [128, 16384], BF16)
        sig = sbuf("sig", [128, NJC_IN * S], BF16)
        hid_addr = nc.lookup_mls("hid").memorylocations[0].addr
        asc_slots = [
            nc.alloc_sbuf_tensor_at("asc0", [128, D], F32,
                                    offset=hid_addr + NTC * D * 2),
            nc.alloc_sbuf_tensor_at("asc1", [128, D], F32,
                                    offset=hid_addr + NTC * D * 2 + 4 * D),
        ]

        NSLOT = min(NTC, 4)
        pM = psum_t("pM", [128, 2048], F32)
        pA = psum_t("pA", [128, 1024], F32)
        pT2b = psum_t("pT2b", [128, 1024], BF16)

        RING_SLOT = 8192
        ring_state = {"idx": 0, "first": True}
        ring_cons = {}

        def ring_load(src_ap, cols, inner, extra_deps=(), queue="sync"):
            i = ring_state["idx"]
            ring_state["idx"] += 1
            deps = list(extra_deps)
            if ring_state["first"]:
                deps += pro_done
                ring_state["first"] = False
            if (i - 2) in ring_cons:
                deps.append(ring_cons[i - 2])
            dst = ring[:, (i % 2) * RING_SLOT:(i % 2) * RING_SLOT + cols]
            dst = dst.rearrange("p (a b) -> p a b", b=inner)
            op = sc.dma(queue, dst, src_ap, deps=deps, key=f"r{i % 2}")
            return i, op

        def ring_ap(i, cols):
            return ring[:, (i % 2) * RING_SLOT:(i % 2) * RING_SLOT + cols]

        q1v = q1s.rearrange("(dc p) j -> p dc j", p=128)
        q2v = q2s.rearrange("(dc p) j -> p dc j", p=128)
        q3v = q3s.rearrange("(jc p) e2 -> p jc e2", p=128)
        wtv = wts.rearrange("(dc p) e2 -> p dc e2", p=128)

        def z_sb(tc):
            return hid[:, tc * D:(tc + 1) * D]

        def emit_ln(x_src_ap, slot, xreg, tc, extra_dep=None):
            ub = ubuf4[:, tc * D:(tc + 1) * D]
            ureg = f"ub{tc}"
            c0 = slot * 8
            s1 = stats[:, c0 + 0:c0 + 1]
            s2 = stats[:, c0 + 1:c0 + 2]
            nmu = stats[:, c0 + 2:c0 + 3]
            mu2 = stats[:, c0 + 3:c0 + 4]
            var = stats[:, c0 + 4:c0 + 5]
            stdv = stats[:, c0 + 5:c0 + 6]
            rstd = stats[:, c0 + 6:c0 + 7]
            sreg = f"stats{slot}"
            dep_x = sc.rdeps(xreg) + ([extra_dep] if extra_dep else [])
            sq = sc.add("vector", lambda e, xa=x_src_ap, ub=ub:
                e.scalar_tensor_tensor(ub, xa, 1.0, xa, ALU.mult, ALU.mult,
                                       accum_out=s2),
                deps=dep_x + sc.wdeps(ureg) + sc.wdeps(sreg))
            sc.note_read(sq, xreg)
            red = sc.add("vector", lambda e, xa=x_src_ap: e.tensor_reduce(
                s1, xa, AX, ALU.add), deps=dep_x + [sq])
            sc.note_read(red, xreg)
            nm = sc.add("vector", lambda e: e.tensor_single_scalar(
                nmu, s1, -1.0 / D, ALU.mult), deps=[red])
            m2 = sc.add("vector", lambda e: e.tensor_mul(mu2, nmu, nmu),
                        deps=[nm])
            vv = sc.add("vector", lambda e: e.scalar_tensor_tensor(
                var, s2, 1.0 / D, mu2, ALU.mult, ALU.subtract),
                deps=[m2, sq])
            sd = sc.add("scalar", lambda e: e.activation(
                stdv, var, AF.Sqrt, bias=cEps[:, 0:1]),
                deps=[vv] + sc.rdeps("cEps"))
            rs = sc.add("vector", lambda e: e.reciprocal(rstd, stdv),
                        deps=[sd])
            ap = sc.add("vector", lambda e, xa=x_src_ap, ub=ub: e.tensor_scalar(
                ub, xa, nmu, rstd, ALU.add, ALU.mult), deps=[rs, sq])
            sc.note_read(ap, xreg)
            sc.wset(ap, ureg)
            sc.wset(ap, sreg)
            return ap

        tpar = [0]

        def emit_transposes(tc, dep_apply):
            last_ev = None
            for f0 in range(0, NDC, 8):
                npc = min(8, NDC - f0)
                par = tpar[0] % 2
                tpar[0] += 1
                pt = pT if par == 0 else pT2b
                preg = "pTa" if par == 0 else "pTb"
                trs = []
                for i in range(npc):
                    tr = sc.add("tensor", lambda e, dc=f0 + i, i=i, tc=tc,
                                pt=pt:
                                e.transpose(
                                    pt[:, i * 128:(i + 1) * 128],
                                    ubuf4[:, tc * D + dc * 128:
                                          tc * D + (dc + 1) * 128],
                                    ident[:]),
                                deps=([dep_apply] + sc.rdeps("ident")
                                      + sc.wdeps(preg)) if i == 0 else [],
                                sem=None if i < npc - 1 else "auto")
                    trs.append(tr)
                sc.wset(trs[-1], preg)
                sc.note_read(trs[-1], f"ub{tc}")

                def mk(e, f0=f0, npc=npc, tc=tc, pt=pt):
                    dst = ufm[:].rearrange("p (dc t) -> p dc t", t=S)[
                        :, f0:f0 + npc, tc * 128:(tc + 1) * 128]
                    src = pt[:, 0:npc * 128].rearrange("p (a b) -> p a b",
                                                       b=128)
                    return e.tensor_copy(dst, src)

                ev = sc.add("vector", mk, deps=[trs[-1]] + sc.wdeps("ufm_w"))
                sc.note_read(ev, preg)
                last_ev = ev
            return last_ev

        def emit_ln_fin(xtile, xreg, tc, parts):
            ub = ubuf4[:, tc * D:(tc + 1) * D]
            ureg = f"ub{tc}"
            slot = tc % NSLOT
            c0 = slot * 8
            s1 = stats[:, c0 + 0:c0 + 1]
            s2 = stats[:, c0 + 1:c0 + 2]
            nmu = stats[:, c0 + 2:c0 + 3]
            mu2 = stats[:, c0 + 3:c0 + 4]
            var = stats[:, c0 + 4:c0 + 5]
            stdv = stats[:, c0 + 5:c0 + 6]
            rstd = stats[:, c0 + 6:c0 + 7]
            sreg = f"stats{slot}"
            f1 = sc.add("vector", lambda e, tc=tc: e.tensor_reduce(
                s1, stats2[:, tc * 8:tc * 8 + NEB], AX, ALU.add),
                deps=parts + sc.wdeps(sreg))
            f2 = sc.add("vector", lambda e, tc=tc: e.tensor_reduce(
                s2, stats2[:, tc * 8 + 4:tc * 8 + 4 + NEB], AX, ALU.add),
                deps=[f1])
            nm = sc.add("vector", lambda e: e.tensor_single_scalar(
                nmu, s1, -1.0 / D, ALU.mult), deps=[f1])
            m2 = sc.add("vector", lambda e: e.tensor_mul(mu2, nmu, nmu),
                        deps=[nm])
            vv = sc.add("vector", lambda e: e.scalar_tensor_tensor(
                var, s2, 1.0 / D, mu2, ALU.mult, ALU.subtract),
                deps=[m2, f2])
            sd = sc.add("scalar", lambda e: e.activation(
                stdv, var, AF.Sqrt, bias=cEps[:, 0:1]),
                deps=[vv] + sc.rdeps("cEps"))
            rs = sc.add("vector", lambda e: e.reciprocal(rstd, stdv),
                        deps=[sd])
            xa = xtile[:, tc * D:(tc + 1) * D]
            ap = sc.add("vector", lambda e, xa=xa, ub=ub: e.tensor_scalar(
                ub, xa, nmu, rstd, ALU.add, ALU.mult),
                deps=[rs] + sc.rdeps(xreg[tc]) + sc.wdeps(ureg))
            sc.note_read(ap, xreg[tc])
            sc.wset(ap, ureg)
            sc.wset(ap, sreg)
            return ap

        def emit_ln_stats_all(xtile, xreg, extra_dep=None):
            return [emit_ln(xtile[:, tc * D:(tc + 1) * D], tc % NSLOT,
                            xreg[tc], tc, extra_dep) for tc in range(NTC)]

        def emit_transposes_all(applies):
            last_ev = None
            for tc in range(NTC):
                last_ev = emit_transposes(tc, applies[tc])
            sc.wset(last_ev, "ufm_w")
            return last_ev

        def emit_ln_all(xtile, xreg, extra_dep=None):
            return emit_transposes_all(emit_ln_stats_all(xtile, xreg,
                                                         extra_dep))

        def ffn_sublayer(xtile, xreg, ufm_dep, ln_hook=None, collect=False,
                         stat_parts=None, pump=0):
            sig_ops = {}
            last_mm = None
            last_hm = None
            for jb in range(NJB):
                if pump:
                    tail_pump(pump)
                i1, ld1 = ring_load(
                    q1v[:, :, jb * JBW:(jb + 1) * JBW], NDC * JBW, JBW,
                    extra_deps=[scratch_done["q1s"]])
                i2, ld2 = ring_load(
                    q2v[:, :, jb * JBW:(jb + 1) * JBW], NDC * JBW, JBW,
                    extra_deps=[scratch_done["q2s"]])
                for half in range(2):
                    fi, ldop = (i1, ld1) if half == 0 else (i2, ld2)
                    wap = ring_ap(fi, NDC * JBW)
                    for jci in range(NJC_IN):
                        jc = jb * NJC_IN + jci
                        bsel = jc % 2
                        ssel = jci
                        pb = (pM[:, bsel * 512:bsel * 512 + S] if half == 0
                              else pA[:, bsel * 512:bsel * 512 + S])
                        breg = (f"pM{bsel}" if half == 0 else f"pA{bsel}")
                        mmdeps = [ldop, ufm_dep] + sc.wdeps(breg)
                        lm = None
                        for dc in range(NDC):
                            lm = sc.add(
                                "tensor",
                                lambda e, dc=dc, jci=jci, wap=wap, pb=pb:
                                e.matmul(
                                    pb,
                                    wap[:, dc * JBW + jci * 128:
                                        dc * JBW + (jci + 1) * 128],
                                    ufm[:, dc * S:(dc + 1) * S],
                                    start=(dc == 0), stop=(dc == NDC - 1)),
                                deps=mmdeps if dc == 0 else [],
                                sem=None if dc < NDC - 1 else "auto")
                        last_mm = lm
                        sc.wset(lm, breg)
                        if jci == NJC_IN - 1:
                            ring_cons[fi] = lm
                        if half == 0:
                            sg = sc.add("scalar",
                                        lambda e, pb=pb, ssel=ssel:
                                        e.activation(
                                            sig[:, ssel * S:(ssel + 1) * S],
                                            pb, AF.Sigmoid),
                                        deps=[lm] + sc.wdeps(f"sig{ssel}"))
                            sc.note_read(sg, breg)
                            sc.wset(sg, f"sig{ssel}")
                            sig_ops[jc] = sg
                        else:
                            hm = sc.add("vector",
                                        lambda e, pb=pb, jc=jc, ssel=ssel:
                                        e.tensor_tensor(
                                            hid[:, jc * S:(jc + 1) * S],
                                            sig[:, ssel * S:(ssel + 1) * S],
                                            pb, ALU.mult),
                                        deps=[sig_ops[jc], lm]
                                        + (sc.wdeps("hid") if jc == 0 else []))
                            sc.note_read(hm, breg, f"sig{ssel}")
                            last_hm = hm
            sc.note_read(last_mm, "ufm_w")
            sc.wset(last_hm, "hid")

            for eb in range(NEB):
                stop_mm = None
                if pump:
                    if eb == 0:
                        tail_drain_until(tail_marks["q3_end"])
                    tail_pump(pump)
                jc0 = 0
                for gsz in pb_parts:
                    if pump:
                        tail_pump(pump)
                    fi, ldop = ring_load(
                        q3v[:, jc0:jc0 + gsz, eb * EBW:(eb + 1) * EBW],
                        gsz * EBW, EBW,
                        extra_deps=[scratch_done["q3s"]], queue="scalar")
                    wap = ring_ap(fi, gsz * EBW)
                    part_last = None
                    for jci in range(gsz):
                        jc = jc0 + jci
                        for tc in range(NTC):
                            first = (jc == 0)
                            lastg = (jc == NJC - 1)
                            deps = []
                            if first:
                                deps = sc.wdeps(f"pM{tc}")
                                if tc == 0:
                                    deps += [last_hm, ldop]
                            elif jci == 0 and tc == 0:
                                deps = [ldop]
                            is_part_last = (jci == gsz - 1 and tc == NTC - 1)
                            mm = sc.add(
                                "tensor",
                                lambda e, jc=jc, tc=tc, jci=jci, wap=wap,
                                first=first, lastg=lastg:
                                e.matmul(
                                    pM[:, tc * 512:tc * 512 + EBW],
                                    hid[:, jc * S + tc * 128:
                                        jc * S + (tc + 1) * 128],
                                    wap[:, jci * EBW:(jci + 1) * EBW],
                                    start=first, stop=lastg),
                                deps=deps,
                                sem="auto" if is_part_last else None)
                            if is_part_last:
                                part_last = mm
                            if lastg and tc == NTC - 1:
                                stop_mm = mm
                    ring_cons[fi] = part_last
                    jc0 += gsz
                sc.note_read(stop_mm, "hid")
                if eb == 0 and ln_hook is not None:
                    ln_hook()
                    ln_hook = None
                for tc in range(NTC):
                    dst = xtile[:, tc * D + eb * EBW:tc * D + (eb + 1) * EBW]
                    ev = sc.add("vector",
                                lambda e, tc=tc, dst=dst:
                                e.scalar_tensor_tensor(
                                    dst, pM[:, tc * 512:tc * 512 + EBW], 0.5,
                                    dst, ALU.mult, ALU.add),
                                deps=[stop_mm] + sc.wdeps(xreg[tc]))
                    sc.note_read(ev, f"pM{tc}")
                    sc.wset(ev, xreg[tc])
                    if collect:
                        gdst = ubuf4[:, tc * D + eb * EBW:
                                     tc * D + (eb + 1) * EBW]
                        s2p = stats2[:, tc * 8 + 4 + eb:tc * 8 + 5 + eb]
                        s1p = stats2[:, tc * 8 + eb:tc * 8 + 1 + eb]
                        psq = sc.add("vector",
                                     lambda e, dst=dst, gdst=gdst, s2p=s2p:
                                     e.scalar_tensor_tensor(
                                         gdst, dst, 1.0, dst, ALU.mult,
                                         ALU.mult, accum_out=s2p),
                                     deps=[ev] + sc.wdeps(f"ub{tc}"))
                        sc.note_read(psq, xreg[tc])
                        sc.wset(psq, f"ub{tc}")
                        prd = sc.add("vector",
                                     lambda e, dst=dst, s1p=s1p:
                                     e.tensor_reduce(s1p, dst, AX, ALU.add),
                                     deps=[ev, psq])
                        sc.note_read(prd, xreg[tc])
                        stat_parts[tc].append(psq)
                        stat_parts[tc].append(prd)
            return

        def attn_ln_chain(tc, xtile, xreg, h_stop):
            asc = asc_slots[tc % 2]
            areg = f"asc{tc % 2}"
            hev = sc.add("scalar", lambda e, asc=asc: e.copy(asc[:], pM[:, 0:D]),
                         deps=[h_stop] + sc.wdeps(areg)
                         + (sc.wdeps("hid") if tc < 2 else []))
            if tc < 2:
                sc.note_read(hev, "hid")
            for b in range(NEB):
                sc.note_read(hev, f"pM{b}")
            sc.wset(hev, areg)
            slot = tc % NSLOT
            c0 = slot * 8
            s1 = stats[:, c0 + 0:c0 + 1]
            s2 = stats[:, c0 + 1:c0 + 2]
            nmu = stats[:, c0 + 2:c0 + 3]
            mu2 = stats[:, c0 + 3:c0 + 4]
            var = stats[:, c0 + 4:c0 + 5]
            stdv = stats[:, c0 + 5:c0 + 6]
            rstd = stats[:, c0 + 6:c0 + 7]
            sreg = f"stats{slot}"
            sgw = min(D, NJC_IN * S)
            sq = sc.add("vector", lambda e, sgw=sgw, asc=asc:
                e.scalar_tensor_tensor(
                    sig[:, 0:sgw], asc[:, 0:sgw], 1.0, asc[:, 0:sgw],
                    ALU.mult, ALU.mult, accum_out=s2),
                deps=[hev] + sc.wdeps(*[f"sig{i}" for i in range(NJC_IN)])
                + sc.wdeps(sreg))
            sc.note_read(sq, areg)
            assert D <= sgw, "attn square garbage dest too small"
            red = sc.add("vector", lambda e, asc=asc: e.tensor_reduce(
                s1, asc[:], AX, ALU.add), deps=[hev, sq])
            sc.note_read(red, areg)
            nm = sc.add("vector", lambda e: e.tensor_single_scalar(
                nmu, s1, -1.0 / D, ALU.mult), deps=[red])
            m2 = sc.add("vector", lambda e: e.tensor_mul(mu2, nmu, nmu),
                        deps=[nm])
            vv = sc.add("vector", lambda e: e.scalar_tensor_tensor(
                var, s2, 1.0 / D, mu2, ALU.mult, ALU.subtract),
                deps=[m2, sq])
            sd = sc.add("scalar", lambda e: e.activation(
                stdv, var, AF.Sqrt, bias=cEps[:, 0:1]),
                deps=[vv] + sc.rdeps("cEps"))
            rs = sc.add("vector", lambda e: e.reciprocal(rstd, stdv),
                        deps=[sd])
            for i in range(NJC_IN):
                sc.wset(sq, f"sig{i}")
            t1 = sc.add("vector", lambda e, asc=asc: e.scalar_tensor_tensor(
                asc[:], asc[:], nmu, Gbc[:], ALU.add, ALU.mult),
                deps=[nm, sq, red] + sc.rdeps("Gbc") + sc.wdeps(areg))
            sc.note_read(t1, "Gbc")
            sc.wset(t1, areg)
            t2 = sc.add("vector", lambda e, asc=asc: e.scalar_tensor_tensor(
                asc[:], asc[:], rstd, Bbc[:], ALU.mult, ALU.add),
                deps=[t1, rs] + sc.rdeps("Bbc"))
            sc.note_read(t2, "Bbc", sreg)
            sc.wset(t2, areg)
            sc.wset(rs, sreg)
            xa = xtile[:, tc * D:(tc + 1) * D]
            ra = sc.add("gpsimd", lambda e, xa=xa, asc=asc:
                        e.tensor_add(xa, xa, asc[:]),
                        deps=[t2] + sc.wdeps(xreg[tc]))
            sc.note_read(ra, areg)
            sc.wset(ra, xreg[tc])
            return ra

        def attn_sublayer(xtile, xreg, ufm_dep, reset_carry,
                          ln3_applies=None):
            if reset_carry:
                cr0 = sc.add("gpsimd", lambda e: e.memset(carry[:], 0.0),
                             deps=sc.wdeps("carry"))
                sc.wset(cr0, "carry")
                cr1 = sc.add("gpsimd", lambda e: e.memset(carry_bf[:], 0.0),
                             deps=sc.wdeps("carry_bf"))
                sc.wset(cr1, "carry_bf")

            last_zev = None
            last_stop = None
            for eq in range(NEB):
                fi, ldop = ring_load(
                    wtv[:, :, eq * EBW:(eq + 1) * EBW], NDC * EBW, EBW,
                    extra_deps=[scratch_done["wts"]])
                wap = ring_ap(fi, NDC * EBW)
                stop_mms = [None] * NTC
                for dc in range(NDC):
                    for tc in range(NTC):
                        deps = []
                        if dc == 0:
                            deps = sc.wdeps(f"pM{tc}")
                            if tc == 0:
                                deps += [ldop, ufm_dep]
                        mm = sc.add(
                            "tensor",
                            lambda e, dc=dc, tc=tc, wap=wap: e.matmul(
                                pM[:, tc * 512:tc * 512 + EBW],
                                ufm[:, dc * S + tc * 128:
                                    dc * S + (tc + 1) * 128],
                                wap[:, dc * EBW:(dc + 1) * EBW],
                                start=(dc == 0), stop=(dc == NDC - 1)),
                            deps=deps,
                            sem=None if dc < NDC - 1 else "auto")
                        if dc == NDC - 1:
                            stop_mms[tc] = mm
                ring_cons[fi] = stop_mms[NTC - 1]
                last_stop = stop_mms[NTC - 1]
                for tc in range(NTC):
                    zev = sc.add("scalar", lambda e, tc=tc, eq=eq:
                                 e.copy(
                                     z_sb(tc)[:, eq * EBW:(eq + 1) * EBW],
                                     pM[:, tc * 512:tc * 512 + EBW]),
                                 deps=[stop_mms[tc]]
                                 + (sc.wdeps("hid") if eq == 0 and tc == 0
                                    else []))
                    sc.note_read(zev, f"pM{tc}")
                    last_zev = zev
            sc.note_read(last_stop, "ufm_w")
            sc.wset(last_zev, "hid")
            z_dep = last_zev

            last_z_reader = None
            for tc in range(NTC):
                cu_prev = None
                for eb in range(NEB):
                    bsel = eb % 2
                    pca = pA[0:K, bsel * 512:bsel * 512 + EBW]
                    mmc = sc.add("tensor", lambda e, eb=eb, tc=tc, pca=pca:
                                 e.matmul(
                                     pca, CUb[:],
                                     z_sb(tc)[:, eb * EBW:(eb + 1) * EBW],
                                     start=True, stop=True),
                                 deps=[z_dep] + sc.rdeps("CUb")
                                 + sc.wdeps(f"pA{bsel}"))
                    sc.wset(mmc, f"pA{bsel}")
                    sc.note_read(mmc, "CUb")
                    cup = sc.add("vector",
                                 lambda e, eb=eb, pca=pca:
                                 e.scalar_tensor_tensor(
                                     carry[:, eb * EBW:(eb + 1) * EBW],
                                     carry[:, eb * EBW:(eb + 1) * EBW],
                                     lam128[:, 0:1], pca,
                                     ALU.mult, ALU.add),
                                 deps=[mmc] + sc.wdeps("carry")
                                 + sc.rdeps("lam128"))
                    sc.note_read(cup, f"pA{bsel}", "lam128")
                    sc.wset(cup, "carry")
                    cu_prev = cup
                last_z_reader = mmc

                for eb in range(NEB):
                    fd = sc.wdeps(f"pM{eb}")
                    if eb == 0:
                        fd += [z_dep] + sc.rdeps("LtT")
                    sc.add("tensor", lambda e, eb=eb, tc=tc: e.matmul(
                        pM[:, eb * 512:eb * 512 + EBW], LtT[:],
                        z_sb(tc)[:, eb * EBW:(eb + 1) * EBW],
                        start=True, stop=False),
                        deps=fd, sem=None)
                cp_deps = sc.rdeps("CPb", "carry_bf")
                h_stop = None
                for eb in range(NEB):
                    h_stop = sc.add("tensor", lambda e, eb=eb: e.matmul(
                        pM[:, eb * 512:eb * 512 + EBW], CPb[:],
                        carry_bf[:, eb * EBW:(eb + 1) * EBW],
                        start=False, stop=True),
                        deps=cp_deps if eb == 0 else [],
                        sem=None if eb < NEB - 1 else "auto")
                sc.note_read(h_stop, "carry_bf", "LtT", "CPb")
                for b in range(NEB):
                    sc.wset(h_stop, f"pM{b}")

                cbf = sc.add("gpsimd", lambda e: e.tensor_copy(
                    carry_bf[:], carry[:]),
                    deps=[cu_prev] + sc.wdeps("carry_bf"))
                sc.note_read(cbf, "carry")
                sc.wset(cbf, "carry_bf")

                attn_ln_chain(tc, xtile, xreg, h_stop)
                if ln3_applies is not None:
                    ln3_applies.append(emit_ln(
                        xtile[:, tc * D:(tc + 1) * D], tc % NSLOT,
                        xreg[tc], tc))
            sc.note_read(last_z_reader, "hid")

        xb1_addr = nc.lookup_mls("xb1").memorylocations[0].addr
        assert NTC * D * 4 >= 14 * D
        t_wb0 = nc.alloc_sbuf_tensor_at("t_wb0", [128, D], F32,
                                        offset=xb1_addr)
        t_wb1 = nc.alloc_sbuf_tensor_at("t_wb1", [128, D], F32,
                                        offset=xb1_addr + 4 * D)
        t_qa = nc.alloc_sbuf_tensor_at("t_qa", [128, D], F32,
                                       offset=xb1_addr + 8 * D)
        t_qc = nc.alloc_sbuf_tensor_at("t_qc", [128, D], BF16,
                                       offset=xb1_addr + 12 * D)
        RACCT = 40
        GT1, GT2 = 41, 42
        tail_done = []

        tail_thunks = []

        def build_tail_thunks():
            wbufs = [t_wb0, t_wb1]
            state = {"bf": [list(pro_done), list(pro_done)], "last_q": None,
                     "gamma_ready": None, "prev": None}
            tmats = [
                ("q3s", w3t_in, H, D, q3s, True, 2, "sg2"),
                ("wts", Wt_in, D, D, wts, False, -1, "sg3"),
            ]
            for (mname, w_ap, NR, NCOL, scratch, doq, rcol, sgkey) in tmats:
                nch = NR // 128
                if doq:
                    def t_z0():
                        z0 = sc.add("vector", lambda e: e.memset(
                            stats[:, RACCT:RACCT + 1], 0.0),
                            deps=list(pro_done))
                        state["prev"] = z0
                    tail_thunks.append(t_z0)
                    for c in range(nch):
                        def t_p1(c=c, w_ap=w_ap, NCOL=NCOL):
                            buf = wbufs[c % 2]
                            ld = sc.dma("sync", buf[:, 0:NCOL],
                                        w_ap[c * 128:(c + 1) * 128, :],
                                        deps=state["bf"][c % 2],
                                        key=f"tl{c % 2}")
                            red = sc.add("vector", lambda e, b=buf, n=NCOL:
                                         e.tensor_reduce(
                                             stats[:, GT1:GT1 + 1],
                                             b[:, 0:n], AX, ALU.add,
                                             apply_absolute_value=True),
                                         deps=[ld, state["prev"]])
                            state["prev"] = sc.add(
                                "vector", lambda e: e.tensor_add(
                                    stats[:, RACCT:RACCT + 1],
                                    stats[:, RACCT:RACCT + 1],
                                    stats[:, GT1:GT1 + 1]), deps=[red])
                            state["bf"][c % 2] = [red]
                        tail_thunks.append(t_p1)

                    def t_gam(NR=NR, NCOL=NCOL, rcol=rcol):
                        tg = sc.add("tensor", lambda e: e.matmul(
                            pA[0:1, 300:301], stats[:, RACCT:RACCT + 1],
                            onescolf[:], start=True, stop=True),
                            deps=[state["prev"]] + sc.wdeps("pA0"))
                        sc.wset(tg, "pA0")
                        gm = sc.add("vector", lambda e, n=NR * NCOL:
                                    e.tensor_scalar(
                                        stats[0:1, GT1:GT1 + 1],
                                        pA[0:1, 300:301], 1.0 / n, Q_EPS,
                                        ALU.mult, ALU.add), deps=[tg])
                        sc.note_read(gm, "pA0")
                        rc = sc.add("vector", lambda e: e.reciprocal(
                            stats[0:1, GT2:GT2 + 1],
                            stats[0:1, GT1:GT1 + 1]), deps=[gm])
                        tb = sc.add("tensor", lambda e: e.matmul(
                            pA[:, 301:302], ones1f[:],
                            stats[0:1, GT2:GT2 + 1],
                            start=True, stop=True),
                            deps=[rc] + sc.wdeps("pA0"))
                        sc.wset(tb, "pA0")
                        gr = sc.add("vector", lambda e, rcol=rcol:
                                    e.tensor_copy(
                                        recip[:, rcol:rcol + 1],
                                        pA[:, 301:302]), deps=[tb])
                        sc.note_read(gr, "pA0")
                        state["gamma_ready"] = gr
                    tail_thunks.append(t_gam)

                for c in range(nch):
                    def t_p2(c=c, w_ap=w_ap, NCOL=NCOL, doq=doq, rcol=rcol,
                             scratch=scratch, sgkey=sgkey, mname=mname,
                             nch=nch):
                        buf = wbufs[c % 2]
                        ld = sc.dma("sync", buf[:, 0:NCOL],
                                    w_ap[c * 128:(c + 1) * 128, :],
                                    deps=state["bf"][c % 2],
                                    key=f"tl{c % 2}")
                        if doq:
                            qa_ = sc.add("vector",
                                         lambda e, b=buf, n=NCOL, rcol=rcol:
                                         e.tensor_scalar(
                                             t_qa[:, 0:n], b[:, 0:n],
                                             recip[:, rcol:rcol + 1], 0.5,
                                             ALU.mult, ALU.is_ge),
                                         deps=[ld, state["gamma_ready"],
                                               state["last_q"]])
                            qb_ = sc.add("vector",
                                         lambda e, b=buf, n=NCOL, rcol=rcol:
                                         e.tensor_scalar(
                                             t_qc[:, 0:n], b[:, 0:n],
                                             recip[:, rcol:rcol + 1], -0.5,
                                             ALU.mult, ALU.is_le),
                                         deps=[ld, state["gamma_ready"],
                                               state["last_q"]])
                            q_ = sc.add("vector", lambda e, n=NCOL:
                                        e.tensor_tensor(
                                            t_qc[:, 0:n], t_qa[:, 0:n],
                                            t_qc[:, 0:n], ALU.subtract),
                                        deps=[qa_, qb_])
                        else:
                            q_ = sc.add("vector", lambda e, b=buf, n=NCOL:
                                        e.tensor_copy(t_qc[:, 0:n],
                                                      b[:, 0:n]),
                                        deps=[ld, state["last_q"]])
                        state["bf"][c % 2] = [q_]
                        st_op = sc.dma("sync",
                                       scratch[c * 128:(c + 1) * 128, :],
                                       t_qc[:, 0:NCOL],
                                       deps=[q_], key=sgkey)
                        state["last_q"] = st_op
                        scratch_done[mname] = st_op
                        if mname == "wts" and c == nch - 1:
                            tail_done.append(st_op)
                    tail_thunks.append(t_p2)
                if mname == "q3s":
                    tail_marks["q3_end"] = len(tail_thunks)

        tail_marks = {}
        build_tail_thunks()
        tail_state = {"emitted": 0}

        def tail_pump(n):
            k = 0
            while tail_thunks and k < n:
                tail_thunks.pop(0)()
                tail_state["emitted"] += 1
                k += 1

        def tail_drain_until(n):
            while tail_thunks and tail_state["emitted"] < n:
                tail_thunks.pop(0)()
                tail_state["emitted"] += 1

        def tail_drain():
            while tail_thunks:
                tail_thunks.pop(0)()
                tail_state["emitted"] += 1

        last_store = None
        y_stores = [None, None]
        x_dep = [None, None]
        pend_ln = [None]

        x_dep[0] = x0_ld
        pend_ln[0] = emit_ln_stats_all(xb[0],
                                       [f"xb0_{t}" for t in range(NTC)],
                                       extra_dep=x0_ld)
        ev0 = emit_transposes_all(pend_ln[0])

        for g in range(NTILE):
            si, ti = divmod(g, NT_SEQ)
            p = g % 2
            t0 = ti * S
            xtile = xb[p]
            xreg = [f"xb{p}_{tc}" for tc in range(NTC)]
            def emit_next_load(g=g):
                si2, ti2 = divmod(g + 1, NT_SEQ)
                pn = (g + 1) % 2
                ld = sc.dma("scalar",
                            xb[pn][:].rearrange("p (tc d) -> p tc d", d=D),
                            x_in[si2, ti2 * S:ti2 * S + S, :].rearrange(
                                "(tc q) d -> q tc d", q=128),
                            deps=sc.wdeps(
                                *[f"xb{pn}_{t}" for t in range(NTC)])
                            + (tail_done if g == 0 else []),
                            key=f"xl{pn}")
                x_dep[pn] = ld
                sc.wset(ld, *[f"xb{pn}_{t}" for t in range(NTC)])

            if g + 1 < NTILE and g != 0:
                emit_next_load()

            if g == 0:
                ev = ev0
            else:
                ev = emit_transposes_all(pend_ln[0])
            stat_parts = [[] for _ in range(NTC)]
            ffn_sublayer(xtile, xreg, ev, collect=True, stat_parts=stat_parts,
                         pump=5 if g == 0 else 0)
            applies = [emit_ln_fin(xtile, xreg, tc, stat_parts[tc])
                       for tc in range(NTC)]
            ev = emit_transposes_all(applies)
            ln3_applies = []
            if g == 0:
                tail_drain()
                if NTILE > 1:
                    emit_next_load()
            attn_sublayer(xtile, xreg, ev, reset_carry=(ti == 0 and g > 0),
                          ln3_applies=ln3_applies)
            ev = emit_transposes_all(ln3_applies)
            if g + 1 < NTILE:
                pn2 = (g + 1) % 2
                xreg_n = [f"xb{pn2}_{t}" for t in range(NTC)]

                def ln_hook(pn2=pn2, xreg_n=xreg_n):
                    pend_ln[0] = emit_ln_stats_all(xb[pn2], xreg_n,
                                                   extra_dep=x_dep[pn2])
                ffn_sublayer(xtile, xreg, ev, ln_hook=ln_hook)
            else:
                ffn_sublayer(xtile, xreg, ev)

            st = sc.dma("scalar",
                        y_out[si, t0:t0 + S, :].rearrange(
                            "(tc q) d -> q tc d", q=128),
                        xtile[:].rearrange("p (tc d) -> p tc d", d=D),
                        deps=sc.rdeps(*xreg), key=f"ys{p}")
            sc.note_read(st, *xreg)
            y_stores[p] = st
            last_store = st

        sc.wait_only("scalar", [s for s in y_stores if s is not None])

        sc.finalize()
        with nc.Block() as block:
            sc.emit_all(block)

    return {e: sum(1 for o in sc.ops if o.engine == e) for e in ENGINES}


def kernel(**kw):
    x = np.asarray(kw["L0_attn_x"], dtype=np.float32)
    W = np.ascontiguousarray(np.asarray(kw["L0_attn_W"], dtype=np.float32))
    lam_logit = np.ascontiguousarray(
        np.asarray(kw["L0_attn_lam_logit"], dtype=np.float32))
    eta = np.ascontiguousarray(np.asarray(kw["L0_attn_eta"], dtype=np.float32))
    gamma = np.ascontiguousarray(
        np.asarray(kw["L0_attn_gamma"], dtype=np.float32))
    beta = np.ascontiguousarray(
        np.asarray(kw["L0_attn_beta"], dtype=np.float32))
    w1t = np.ascontiguousarray(np.asarray(kw["L0_ffn_w1"], dtype=np.float32).T)
    w2t = np.ascontiguousarray(np.asarray(kw["L0_ffn_w2"], dtype=np.float32).T)
    w3t = np.ascontiguousarray(np.asarray(kw["L0_ffn_w3"], dtype=np.float32).T)
    Wt = np.ascontiguousarray(W.T)

    B = x.shape[0]
    n_cores = 8
    bl = B // n_cores
    cfg = dict(CFG_FULL)
    cfg.update(B_loc=bl, T=x.shape[1], D=x.shape[2], H=w1t.shape[1])

    nc = bass.Bass("TRN2", target_bir_lowering=False)
    build_program(nc, cfg)
    nc.finalize()

    in_maps = []
    for c in range(n_cores):
        in_maps.append({
            "x": np.ascontiguousarray(x[c * bl:(c + 1) * bl]),
            "Wt": Wt, "lam_logit": lam_logit, "eta": eta,
            "gamma": gamma, "beta": beta,
            "w1t": w1t, "w2t": w2t, "w3t": w3t,
        })
    res = run_bass_kernel_spmd(nc, in_maps, core_ids=list(range(n_cores)))
    return np.concatenate([np.asarray(r["y"]) for r in res.results], axis=0)


# version 17
# speedup vs baseline: 1.0607x; 1.0371x over previous
import numpy as np
import ml_dtypes
from contextlib import ExitStack

import concourse.bass as bass
import concourse.mybir as mybir
from concourse.bass_utils import run_bass_kernel_spmd

F32 = mybir.dt.float32
BF16 = mybir.dt.bfloat16
ALU = mybir.AluOpType
AF = mybir.ActivationFunctionType
AX = mybir.AxisListType.X

LN_EPS = 1e-5
Q_EPS = 1e-8
TC = 128

ENGINES = ("sync", "scalar", "vector", "tensor", "gpsimd")

CFG_FULL = dict(B_loc=2, T=2048, D=2048, H=5632, S=512)


class Op:
    __slots__ = ("engine", "emit", "deps", "sem", "inc_n", "val")

    def __init__(self, engine, emit, deps, sem, inc_n):
        self.engine = engine
        self.emit = emit
        self.deps = deps
        self.sem = sem
        self.inc_n = inc_n
        self.val = None


class Reg:
    __slots__ = ("w", "readers")

    def __init__(self):
        self.w = None
        self.readers = []


class Sched:
    def __init__(self, sems, dma_sems):
        self.sems = sems
        self.dma_sems = dma_sems
        self.ops = []
        self.regs = {}

    def reg(self, name):
        if name not in self.regs:
            self.regs[name] = Reg()
        return self.regs[name]

    def rdeps(self, *names):
        d = []
        for n in names:
            r = self.reg(n)
            if r.w is not None:
                d.append(r.w)
        return d

    def wdeps(self, *names):
        d = []
        for n in names:
            r = self.reg(n)
            d.extend(r.readers)
            if r.w is not None:
                d.append(r.w)
        return d

    def note_read(self, op, *names):
        for n in names:
            self.reg(n).readers.append(op)

    def wset(self, op, *names):
        for n in names:
            r = self.reg(n)
            r.w = op
            r.readers = []

    def add(self, engine, emit, deps=(), sem="auto"):
        if sem == "auto":
            s, n = self.sems[engine], 1
        elif sem is None:
            s, n = None, 0
        else:
            s, n = sem
        deps = [d for d in deps if d is not None]
        op = Op(engine, emit, deps, s, n)
        self.ops.append(op)
        return op

    def dma(self, queue, dst, src, deps=(), key=None):
        s = self.dma_sems[key if key is not None else queue]
        return self.add(queue, lambda e, d=dst, s_=src: e.dma_start(d, s_),
                        deps=deps, sem=(s, 16))

    def wait_only(self, engine, deps):
        return self.add(engine, lambda e: None, deps=deps, sem=None)

    def finalize(self):
        counts = {}
        for op in self.ops:
            if op.sem is not None:
                k = id(op.sem)
                counts[k] = counts.get(k, 0) + op.inc_n
                op.val = counts[k]

    def emit_all(self, block):
        for eng_name in ENGINES:
            ops = [o for o in self.ops if o.engine == eng_name]
            if not ops:
                continue

            def body(eng, ops=ops):
                waited = {}
                for op in ops:
                    for d in op.deps:
                        assert d.sem is not None, "dep on op without sem"
                        k = id(d.sem)
                        if waited.get(k, -1) >= d.val:
                            continue
                        eng.wait_ge(d.sem, d.val)
                        waited[k] = d.val
                    inst = op.emit(eng)
                    if op.sem is not None:
                        inst.then_inc(op.sem, op.inc_n)

            getattr(block, eng_name)(body)


def build_program(nc: bass.Bass, cfg):
    B_loc, T, D, H, S = (cfg[k] for k in ("B_loc", "T", "D", "H", "S"))
    NTC = S // TC
    NDC = D // 128
    NJC = H // 128
    JBW = min(H, 512)
    NJB = H // JBW
    NJC_IN = JBW // 128
    EBW = min(D, 512)
    NEB = D // EBW
    NT_SEQ = T // S
    NTILE = B_loc * NT_SEQ
    K = 4
    assert S % TC == 0 and D % 128 == 0 and H % JBW == 0 and D % EBW == 0

    max_g = max(1, 8192 // EBW)
    pb_parts = []
    r = NJC
    while r > 0:
        g = min(max_g, r)
        pb_parts.append(g)
        r -= g

    x_in = nc.dram_tensor("x", [B_loc, T, D], F32, kind="ExternalInput").ap()
    Wt_in = nc.dram_tensor("Wt", [D, D], F32, kind="ExternalInput").ap()
    lam_in = nc.dram_tensor("lam_logit", [K], F32, kind="ExternalInput").ap()
    eta_in = nc.dram_tensor("eta", [K], F32, kind="ExternalInput").ap()
    gamma_in = nc.dram_tensor("gamma", [D], F32, kind="ExternalInput").ap()
    beta_in = nc.dram_tensor("beta", [D], F32, kind="ExternalInput").ap()
    w1t_in = nc.dram_tensor("w1t", [D, H], F32, kind="ExternalInput").ap()
    w2t_in = nc.dram_tensor("w2t", [D, H], F32, kind="ExternalInput").ap()
    w3t_in = nc.dram_tensor("w3t", [H, D], F32, kind="ExternalInput").ap()
    y_out = nc.dram_tensor("y", [B_loc, T, D], F32, kind="ExternalOutput").ap()

    q1s = nc.dram_tensor("q1s", [D, H], BF16, kind="Internal").ap()
    q2s = nc.dram_tensor("q2s", [D, H], BF16, kind="Internal").ap()
    q3s = nc.dram_tensor("q3s", [H, D], BF16, kind="Internal").ap()
    wts = nc.dram_tensor("wts", [D, D], BF16, kind="Internal").ap()

    with ExitStack() as ctx:
        def sbuf(name, shape, dt=F32):
            return ctx.enter_context(nc.sbuf_tensor(name, shape, dt))

        def psum_t(name, shape, dt=F32):
            return ctx.enter_context(nc.psum_tensor(name, shape, dt))

        def sem(name):
            return ctx.enter_context(nc.semaphore(name))

        sems = {e: sem("s_" + e) for e in ENGINES}
        dma_keys = ("par", "sg0", "sg1", "sg2", "sg3", "wl0", "wl1",
                    "r0a", "r0b", "r1a", "r1b",
                    "xl0", "xl1", "ys0", "ys1", "tl0", "tl1")
        dma_sems = {k: sem("sd_" + k) for k in dma_keys}
        sc = Sched(sems, dma_sems)

        pT = psum_t("pT", [128, 1024], BF16)

        Gbc = sbuf("Gbc", [128, D])
        Bbc = sbuf("Bbc", [128, D])
        carry = sbuf("carry", [K, D])
        carry_bf = sbuf("carry_bf", [K, D], BF16)
        LtT = sbuf("LtT", [TC, TC], BF16)
        CPb = sbuf("CPb", [K, TC], BF16)
        CUb = sbuf("CUb", [TC, K], BF16)
        lam128 = sbuf("lam128", [K, 1])
        ident = sbuf("ident", [128, 128], BF16)
        ones1f = sbuf("ones1f", [1, 128])
        onescolf = sbuf("onescolf", [128, 1])
        stats = sbuf("stats", [128, 48])
        stats2 = sbuf("stats2", [128, 32])
        recip = sbuf("recip", [128, 3])
        cEps = sbuf("cEps", [128, 1])
        cTC = sbuf("cTC", [128, 1])

        scratch_done = {}

        xb0_t = sbuf("xb0", [128, (S // TC) * D])
        ubuf4 = sbuf("ubuf4", [128, (S // TC) * D], BF16)
        ufm = sbuf("ufm", [128, (D // 128) * S], BF16)

        MX = max(D, H)
        with nc.sbuf_tensor("pro_w0", [128, MX], F32) as pro_w0, \
             nc.sbuf_tensor("pro_w1", [128, MX], F32) as pro_w1, \
             nc.sbuf_tensor("pro_qa", [128, MX], F32) as pro_qa, \
             nc.sbuf_tensor("pro_qb", [128, MX], F32) as pro_qb, \
             nc.sbuf_tensor("pro_qc", [128, MX], BF16) as pro_qc, \
             nc.sbuf_tensor("pro_m", [128, 1024], F32) as pro_m, \
             nc.psum_tensor("pPro", [128, 2048], F32) as pPro:

            LAMR, ETAR, LNLR = 0, 4, 8
            LAMC, LNLC, ETAC = 12, 13, 14
            LNL128, ETA128 = 16, 20
            ICOL, ICOLR, RACC, GTMP, GTMP2, CUS = 24, 25, 26, 27, 28, 29
            M0, TS, MSK, ACC, EK, TK, IROW = 32, 160, 288, 416, 544, 672, 800

            sc.dma("scalar", pro_m[0:1, LAMR:LAMR + K], lam_in[None, :],
                   key="par")
            sc.dma("scalar", pro_m[0:1, ETAR:ETAR + K], eta_in[None, :],
                   key="par")
            sc.dma("scalar", pro_m[0:K, LAMC:LAMC + 1], lam_in[:, None],
                   key="par")
            sc.dma("scalar", pro_m[0:K, ETAC:ETAC + 1], eta_in[:, None],
                   key="par")
            sc.dma("scalar", pro_w0[0:1, 0:D], gamma_in[None, :], key="par")
            d_params = sc.dma("scalar", pro_w1[0:1, 0:D], beta_in[None, :],
                              key="par")
            d_lam_r = d_eta_r = d_lam_c = d_eta_c = d_gam = d_bet = d_params

            g_ic = sc.add("gpsimd", lambda e: e.iota(
                pro_m[:, ICOL:ICOL + 1], [[0, 1]], channel_multiplier=1,
                allow_small_or_imprecise_dtypes=True))
            g_icr = sc.add("gpsimd", lambda e: e.iota(
                pro_m[:, ICOLR:ICOLR + 1], [[0, 1]], base=TC - 1,
                channel_multiplier=-1, allow_small_or_imprecise_dtypes=True))
            g_ir = sc.add("gpsimd", lambda e: e.iota(
                pro_m[0:1, IROW:IROW + 128], [[1, 128]], channel_multiplier=0,
                allow_small_or_imprecise_dtypes=True))
            g_o1 = sc.add("gpsimd", lambda e: e.memset(ones1f[:], 1.0))
            g_oc = sc.add("gpsimd", lambda e: e.memset(onescolf[:], 1.0))
            g_c0 = sc.add("gpsimd", lambda e: e.memset(carry[:], 0.0))
            g_c1 = sc.add("gpsimd", lambda e: e.memset(carry_bf[:], 0.0))
            g_ce = sc.add("gpsimd", lambda e: e.memset(cEps[:], LN_EPS))
            g_ct = sc.add("gpsimd", lambda e: e.memset(cTC[:], float(TC)))
            sc.wset(g_ce, "cEps")
            sc.wset(g_ct, "cTC")
            sc.wset(g_c0, "carry")
            sc.wset(g_c1, "carry_bf")

            a_sr = sc.add("scalar", lambda e: e.activation(
                pro_m[0:1, LAMR:LAMR + K], pro_m[0:1, LAMR:LAMR + K],
                AF.Sigmoid), deps=[d_lam_r])
            a_sc = sc.add("scalar", lambda e: e.activation(
                pro_m[0:K, LAMC:LAMC + 1], pro_m[0:K, LAMC:LAMC + 1],
                AF.Sigmoid), deps=[d_lam_c])
            a_lr = sc.add("scalar", lambda e: e.activation(
                pro_m[0:1, LNLR:LNLR + K], pro_m[0:1, LAMR:LAMR + K], AF.Ln),
                deps=[a_sr])
            a_lc = sc.add("scalar", lambda e: e.activation(
                pro_m[0:K, LNLC:LNLC + 1], pro_m[0:K, LAMC:LAMC + 1], AF.Ln),
                deps=[a_sc])
            a_l128 = sc.add("scalar", lambda e: e.activation(
                lam128[:, 0:1], pro_m[0:K, LNLC:LNLC + 1], AF.Exp,
                scale=cTC[0:K, 0:1]), deps=[a_lc, g_ct])
            sc.wset(a_l128, "lam128")

            def bcast(dst_ap, src_row_ap, width, deps):
                t = None
                first_deps = list(deps) + [g_o1] + sc.wdeps("pPro")
                for c0 in range(0, width, 512):
                    cw = min(512, width - c0)
                    t = sc.add("tensor",
                               lambda e, s=src_row_ap, c0=c0, cw=cw: e.matmul(
                                   pPro[:, c0:c0 + cw], ones1f[:],
                                   s[:, c0:c0 + cw], start=True, stop=True),
                               deps=first_deps if c0 == 0 else [],
                               sem=None if c0 + cw < width else "auto")
                sc.wset(t, "pPro")
                v = sc.add("vector",
                           lambda e, d=dst_ap, w=width: e.tensor_copy(
                               d, pPro[:, 0:w]), deps=[t])
                sc.note_read(v, "pPro")
                return v

            v_m0 = bcast(pro_m[:, M0:M0 + 128], pro_m[0:1, IROW:IROW + 128],
                         128, [g_ir])
            v_l128 = bcast(pro_m[:, LNL128:LNL128 + K],
                           pro_m[0:1, LNLR:LNLR + K], K, [a_lr])
            v_e128 = bcast(pro_m[:, ETA128:ETA128 + K],
                           pro_m[0:1, ETAR:ETAR + K], K, [d_eta_r])
            v_gbc = bcast(Gbc[:], pro_w0[0:1, 0:D], D, [d_gam])
            sc.wset(v_gbc, "Gbc")
            v_bbc = bcast(Bbc[:], pro_w1[0:1, 0:D], D, [d_bet])
            sc.wset(v_bbc, "Bbc")

            v_ts = sc.add("vector", lambda e: e.tensor_scalar(
                pro_m[:, TS:TS + 128], pro_m[:, M0:M0 + 128],
                pro_m[:, ICOL:ICOL + 1], None, ALU.subtract),
                deps=[v_m0, g_ic])
            v_msk = sc.add("vector", lambda e: e.tensor_single_scalar(
                pro_m[:, MSK:MSK + 128], pro_m[:, TS:TS + 128], 0.0,
                ALU.is_ge), deps=[v_ts])
            v_eq = sc.add("vector", lambda e: e.tensor_single_scalar(
                pro_m[:, ACC:ACC + 128], pro_m[:, TS:TS + 128], 0.0,
                ALU.is_equal), deps=[v_msk])
            v_id = sc.add("vector", lambda e: e.tensor_copy(
                ident[:], pro_m[:, ACC:ACC + 128]), deps=[v_eq])
            sc.wset(v_id, "ident")
            v_tsc = sc.add("vector", lambda e: e.tensor_scalar_max(
                pro_m[:, TS:TS + 128], pro_m[:, TS:TS + 128], 0.0),
                deps=[v_eq, v_id])

            last = v_tsc
            for k in range(K):
                ek = sc.add("scalar", lambda e, k=k: e.activation(
                    pro_m[:, EK:EK + 128], pro_m[:, TS:TS + 128], AF.Exp,
                    scale=pro_m[:, LNL128 + k:LNL128 + k + 1]),
                    deps=[last, v_l128, v_tsc, a_l128])
                tk = sc.add("vector", lambda e, k=k: e.scalar_tensor_tensor(
                    pro_m[:, TK:TK + 128], pro_m[:, EK:EK + 128],
                    pro_m[:, ETA128 + k:ETA128 + k + 1],
                    pro_m[:, MSK:MSK + 128], ALU.mult, ALU.mult),
                    deps=[ek, v_e128, v_msk])
                last = sc.add("vector", lambda e: e.tensor_add(
                    pro_m[:, ACC:ACC + 128], pro_m[:, ACC:ACC + 128],
                    pro_m[:, TK:TK + 128]), deps=[tk])
            v_lt = sc.add("vector", lambda e: e.tensor_copy(
                LtT[:], pro_m[:, ACC:ACC + 128]), deps=[last])
            sc.wset(v_lt, "LtT")

            a_cp = sc.add("scalar", lambda e: e.activation(
                pro_m[0:K, EK:EK + TC], pro_m[0:K, M0:M0 + TC], AF.Exp,
                scale=pro_m[0:K, LNLC:LNLC + 1],
                bias=pro_m[0:K, LNLC:LNLC + 1]),
                deps=[a_lc, v_m0, last])
            v_cp1 = sc.add("vector", lambda e: e.tensor_scalar(
                pro_m[0:K, TK:TK + TC], pro_m[0:K, EK:EK + TC],
                pro_m[0:K, ETAC:ETAC + 1], None, ALU.mult),
                deps=[a_cp, d_eta_c, v_lt])
            v_cp = sc.add("vector", lambda e: e.tensor_copy(
                CPb[:], pro_m[0:K, TK:TK + TC]), deps=[v_cp1])
            sc.wset(v_cp, "CPb")

            a_cu = sc.add("scalar", lambda e: e.activation(
                pro_m[:, CUS:CUS + K], pro_m[:, LNL128:LNL128 + K], AF.Exp,
                scale=pro_m[:, ICOLR:ICOLR + 1]),
                deps=[v_l128, g_icr, a_cp])
            v_cu = sc.add("vector", lambda e: e.tensor_copy(
                CUb[:], pro_m[:, CUS:CUS + K]), deps=[a_cu, v_cp])
            sc.wset(v_cu, "CUb")

            x0_ld = sc.dma("scalar",
                           xb0_t[:].rearrange("p (tc d) -> p tc d", d=D),
                           x_in[0, 0:S, :].rearrange(
                               "(tc q) d -> q tc d", q=128),
                           deps=[], key="xl0")
            sc.wset(x0_ld, *[f"xb0_{t}" for t in range(S // TC)])

            mats = [
                ("q1s", w1t_in, D, H, q1s, True, 0, "sg0"),
                ("q2s", w2t_in, D, H, q2s, True, 1, "sg1"),
            ]
            wbufs = [pro_w0, pro_w1]
            buf_free = [[v_gbc], [v_bbc]]
            last_qc = None
            qc_free = v_cu

            for (mname, w_ap, NR, NCOL, scratch, doq, rcol, sgkey) in mats:
                nch = NR // 128
                gamma_ready = None
                if doq:
                    z0 = sc.add("gpsimd", lambda e: e.memset(
                        pro_m[:, RACC:RACC + 1], 0.0), deps=sc.wdeps("racc"))
                    sc.wset(z0, "racc")
                    prev = z0
                    for c in range(nch):
                        buf = wbufs[c % 2]
                        ld = sc.dma("sync", buf[:, 0:NCOL],
                                    w_ap[c * 128:(c + 1) * 128, :],
                                    deps=buf_free[c % 2], key=f"wl{c % 2}")
                        red = sc.add("vector", lambda e, b=buf, n=NCOL:
                                     e.tensor_reduce(
                                         pro_m[:, GTMP:GTMP + 1], b[:, 0:n],
                                         AX, ALU.add,
                                         apply_absolute_value=True),
                                     deps=[ld, prev])
                        prev = sc.add("vector", lambda e: e.tensor_add(
                            pro_m[:, RACC:RACC + 1], pro_m[:, RACC:RACC + 1],
                            pro_m[:, GTMP:GTMP + 1]), deps=[red])
                        buf_free[c % 2] = [red]
                    tg = sc.add("tensor", lambda e: e.matmul(
                        pPro[0:1, 300:301], pro_m[:, RACC:RACC + 1],
                        onescolf[:], start=True, stop=True),
                        deps=[prev, g_oc] + sc.wdeps("pPro"))
                    sc.wset(tg, "pPro")
                    sc.note_read(tg, "racc")
                    gm = sc.add("vector", lambda e, n=NR * NCOL:
                                e.tensor_scalar(
                                    pro_m[0:1, GTMP:GTMP + 1],
                                    pPro[0:1, 300:301], 1.0 / n, Q_EPS,
                                    ALU.mult, ALU.add), deps=[tg])
                    sc.note_read(gm, "pPro")
                    rc = sc.add("vector", lambda e: e.reciprocal(
                        pro_m[0:1, GTMP2:GTMP2 + 1],
                        pro_m[0:1, GTMP:GTMP + 1]), deps=[gm])
                    gamma_ready = bcast(recip[:, rcol:rcol + 1],
                                        pro_m[0:1, GTMP2:GTMP2 + 1], 1, [rc])

                for c in range(nch):
                    buf = wbufs[c % 2]
                    ld = sc.dma("sync", buf[:, 0:NCOL],
                                w_ap[c * 128:(c + 1) * 128, :],
                                deps=buf_free[c % 2], key=f"wl{c % 2}")
                    if doq:
                        qa = sc.add("vector",
                                    lambda e, b=buf, n=NCOL, rcol=rcol:
                                    e.tensor_scalar(
                                        pro_qa[:, 0:n], b[:, 0:n],
                                        recip[:, rcol:rcol + 1], 0.5,
                                        ALU.mult, ALU.is_ge),
                                    deps=[ld, gamma_ready, last_qc])
                        qb = sc.add("vector",
                                    lambda e, b=buf, n=NCOL, rcol=rcol:
                                    e.tensor_scalar(
                                        pro_qb[:, 0:n], b[:, 0:n],
                                        recip[:, rcol:rcol + 1], -0.5,
                                        ALU.mult, ALU.is_le),
                                    deps=[ld, gamma_ready, last_qc])
                        qc = sc.add("gpsimd", lambda e, n=NCOL: e.tensor_sub(
                            pro_qc[:, 0:n], pro_qa[:, 0:n], pro_qb[:, 0:n]),
                            deps=[qa, qb, qc_free])
                    else:
                        qc = sc.add("gpsimd", lambda e, b=buf, n=NCOL:
                                    e.tensor_copy(pro_qc[:, 0:n], b[:, 0:n]),
                                    deps=[ld, qc_free, last_qc])
                    last_qc = qc
                    buf_free[c % 2] = [qc]
                    st_op = sc.dma("scalar",
                                   scratch[c * 128:(c + 1) * 128, :],
                                   pro_qc[:, 0:NCOL],
                                   deps=[qc], key=sgkey)
                    qc_free = st_op
                    scratch_done[mname] = st_op

            pro_done = [scratch_done[m] for m in ("q1s", "q2s")]
            pro_done += [v_gbc, v_bbc, v_lt, v_cp, v_cu, a_l128, g_c0, g_c1,
                         qc_free, last_qc]


        xb = [xb0_t, sbuf("xb1", [128, NTC * D])]
        HIDW = max(NJC * S, NTC * D + 4 * D)
        hid = sbuf("hid", [128, HIDW], BF16)
        ring = sbuf("ring", [128, 16384], BF16)
        sig = sbuf("sig", [128, NJC_IN * S], BF16)
        hid_addr = nc.lookup_mls("hid").memorylocations[0].addr
        asc_slots = [
            nc.alloc_sbuf_tensor_at("asc0", [128, D], F32,
                                    offset=hid_addr + NTC * D * 2),
            nc.alloc_sbuf_tensor_at("asc1", [128, D], F32,
                                    offset=hid_addr + NTC * D * 2 + 4 * D),
        ]

        NSLOT = min(NTC, 4)
        pM = psum_t("pM", [128, 2048], F32)
        pA = psum_t("pA", [128, 1024], F32)
        pT2b = psum_t("pT2b", [128, 1024], BF16)

        RING_SLOT = 8192
        ring_state = {"idx": 0, "first": True}
        ring_cons = {}

        def ring_load(src_ap, cols, inner, extra_deps=(), queue="sync"):
            i = ring_state["idx"]
            ring_state["idx"] += 1
            deps = list(extra_deps)
            if ring_state["first"]:
                deps += pro_done
                ring_state["first"] = False
            if (i - 2) in ring_cons:
                deps.append(ring_cons[i - 2])
            nblk = cols // inner
            na = (nblk + 1) // 2
            base = (i % 2) * RING_SLOT
            dst_a = ring[:, base:base + na * inner].rearrange(
                "p (a b) -> p a b", b=inner)
            op_a = sc.dma(queue, dst_a, src_ap[:, 0:na, :], deps=deps,
                          key=f"r{i % 2}a")
            if na < nblk:
                dst_b = ring[:, base + na * inner:base + cols].rearrange(
                    "p (a b) -> p a b", b=inner)
                op_b = sc.dma(queue, dst_b, src_ap[:, na:nblk, :],
                              deps=deps, key=f"r{i % 2}b")
            else:
                op_b = op_a
            ring_state["half"] = na
            return i, op_a, op_b

        def ring_ap(i, cols):
            return ring[:, (i % 2) * RING_SLOT:(i % 2) * RING_SLOT + cols]

        q1v = q1s.rearrange("(dc p) j -> p dc j", p=128)
        q2v = q2s.rearrange("(dc p) j -> p dc j", p=128)
        q3v = q3s.rearrange("(jc p) e2 -> p jc e2", p=128)
        wtv = wts.rearrange("(dc p) e2 -> p dc e2", p=128)

        def z_sb(tc):
            return hid[:, tc * D:(tc + 1) * D]

        def emit_ln(x_src_ap, slot, xreg, tc, extra_dep=None):
            ub = ubuf4[:, tc * D:(tc + 1) * D]
            ureg = f"ub{tc}"
            c0 = slot * 8
            s1 = stats[:, c0 + 0:c0 + 1]
            s2 = stats[:, c0 + 1:c0 + 2]
            nmu = stats[:, c0 + 2:c0 + 3]
            mu2 = stats[:, c0 + 3:c0 + 4]
            var = stats[:, c0 + 4:c0 + 5]
            stdv = stats[:, c0 + 5:c0 + 6]
            rstd = stats[:, c0 + 6:c0 + 7]
            sreg = f"stats{slot}"
            dep_x = sc.rdeps(xreg) + ([extra_dep] if extra_dep else [])
            sq = sc.add("vector", lambda e, xa=x_src_ap, ub=ub:
                e.scalar_tensor_tensor(ub, xa, 1.0, xa, ALU.mult, ALU.mult,
                                       accum_out=s2),
                deps=dep_x + sc.wdeps(ureg) + sc.wdeps(sreg))
            sc.note_read(sq, xreg)
            red = sc.add("vector", lambda e, xa=x_src_ap: e.tensor_reduce(
                s1, xa, AX, ALU.add), deps=dep_x + [sq])
            sc.note_read(red, xreg)
            nm = sc.add("vector", lambda e: e.tensor_single_scalar(
                nmu, s1, -1.0 / D, ALU.mult), deps=[red])
            m2 = sc.add("vector", lambda e: e.tensor_mul(mu2, nmu, nmu),
                        deps=[nm])
            vv = sc.add("vector", lambda e: e.scalar_tensor_tensor(
                var, s2, 1.0 / D, mu2, ALU.mult, ALU.subtract),
                deps=[m2, sq])
            sd = sc.add("scalar", lambda e: e.activation(
                stdv, var, AF.Sqrt, bias=cEps[:, 0:1]),
                deps=[vv] + sc.rdeps("cEps"))
            rs = sc.add("vector", lambda e: e.reciprocal(rstd, stdv),
                        deps=[sd])
            ap = sc.add("vector", lambda e, xa=x_src_ap, ub=ub: e.tensor_scalar(
                ub, xa, nmu, rstd, ALU.add, ALU.mult), deps=[rs, sq])
            sc.note_read(ap, xreg)
            sc.wset(ap, ureg)
            sc.wset(ap, sreg)
            return ap

        tpar = [0]

        def emit_transposes(tc, dep_apply):
            last_ev = None
            for f0 in range(0, NDC, 8):
                npc = min(8, NDC - f0)
                par = tpar[0] % 2
                tpar[0] += 1
                pt = pT if par == 0 else pT2b
                preg = "pTa" if par == 0 else "pTb"
                trs = []
                for i in range(npc):
                    tr = sc.add("tensor", lambda e, dc=f0 + i, i=i, tc=tc,
                                pt=pt:
                                e.transpose(
                                    pt[:, i * 128:(i + 1) * 128],
                                    ubuf4[:, tc * D + dc * 128:
                                          tc * D + (dc + 1) * 128],
                                    ident[:]),
                                deps=([dep_apply] + sc.rdeps("ident")
                                      + sc.wdeps(preg)) if i == 0 else [],
                                sem=None if i < npc - 1 else "auto")
                    trs.append(tr)
                sc.wset(trs[-1], preg)
                sc.note_read(trs[-1], f"ub{tc}")

                def mk(e, f0=f0, npc=npc, tc=tc, pt=pt):
                    dst = ufm[:].rearrange("p (dc t) -> p dc t", t=S)[
                        :, f0:f0 + npc, tc * 128:(tc + 1) * 128]
                    src = pt[:, 0:npc * 128].rearrange("p (a b) -> p a b",
                                                       b=128)
                    return e.tensor_copy(dst, src)

                ev = sc.add("vector", mk, deps=[trs[-1]] + sc.wdeps("ufm_w"))
                sc.note_read(ev, preg)
                last_ev = ev
            return last_ev

        def emit_ln_fin(xtile, xreg, tc, parts):
            ub = ubuf4[:, tc * D:(tc + 1) * D]
            ureg = f"ub{tc}"
            slot = tc % NSLOT
            c0 = slot * 8
            s1 = stats[:, c0 + 0:c0 + 1]
            s2 = stats[:, c0 + 1:c0 + 2]
            nmu = stats[:, c0 + 2:c0 + 3]
            mu2 = stats[:, c0 + 3:c0 + 4]
            var = stats[:, c0 + 4:c0 + 5]
            stdv = stats[:, c0 + 5:c0 + 6]
            rstd = stats[:, c0 + 6:c0 + 7]
            sreg = f"stats{slot}"
            f1 = sc.add("vector", lambda e, tc=tc: e.tensor_reduce(
                s1, stats2[:, tc * 8:tc * 8 + NEB], AX, ALU.add),
                deps=parts + sc.wdeps(sreg))
            f2 = sc.add("vector", lambda e, tc=tc: e.tensor_reduce(
                s2, stats2[:, tc * 8 + 4:tc * 8 + 4 + NEB], AX, ALU.add),
                deps=[f1])
            nm = sc.add("vector", lambda e: e.tensor_single_scalar(
                nmu, s1, -1.0 / D, ALU.mult), deps=[f1])
            m2 = sc.add("vector", lambda e: e.tensor_mul(mu2, nmu, nmu),
                        deps=[nm])
            vv = sc.add("vector", lambda e: e.scalar_tensor_tensor(
                var, s2, 1.0 / D, mu2, ALU.mult, ALU.subtract),
                deps=[m2, f2])
            sd = sc.add("scalar", lambda e: e.activation(
                stdv, var, AF.Sqrt, bias=cEps[:, 0:1]),
                deps=[vv] + sc.rdeps("cEps"))
            rs = sc.add("vector", lambda e: e.reciprocal(rstd, stdv),
                        deps=[sd])
            xa = xtile[:, tc * D:(tc + 1) * D]
            ap = sc.add("vector", lambda e, xa=xa, ub=ub: e.tensor_scalar(
                ub, xa, nmu, rstd, ALU.add, ALU.mult),
                deps=[rs] + sc.rdeps(xreg[tc]) + sc.wdeps(ureg))
            sc.note_read(ap, xreg[tc])
            sc.wset(ap, ureg)
            sc.wset(ap, sreg)
            return ap

        def emit_ln_stats_all(xtile, xreg, extra_dep=None):
            return [emit_ln(xtile[:, tc * D:(tc + 1) * D], tc % NSLOT,
                            xreg[tc], tc, extra_dep) for tc in range(NTC)]

        def emit_transposes_all(applies):
            last_ev = None
            for tc in range(NTC):
                last_ev = emit_transposes(tc, applies[tc])
            sc.wset(last_ev, "ufm_w")
            return last_ev

        def emit_ln_all(xtile, xreg, extra_dep=None):
            return emit_transposes_all(emit_ln_stats_all(xtile, xreg,
                                                         extra_dep))

        def ffn_sublayer(xtile, xreg, ufm_dep, ln_hook=None, collect=False,
                         stat_parts=None, pump=0):
            sig_ops = {}
            last_mm = None
            last_hm = None
            for jb in range(NJB):
                if pump:
                    tail_pump(pump)
                i1, ld1a, ld1b = ring_load(
                    q1v[:, :, jb * JBW:(jb + 1) * JBW], NDC * JBW, JBW,
                    extra_deps=[scratch_done["q1s"]])
                i2, ld2a, ld2b = ring_load(
                    q2v[:, :, jb * JBW:(jb + 1) * JBW], NDC * JBW, JBW,
                    extra_deps=[scratch_done["q2s"]])
                half_dc = (NDC + 1) // 2
                for half in range(2):
                    fi, lda, ldb = ((i1, ld1a, ld1b) if half == 0
                                    else (i2, ld2a, ld2b))
                    wap = ring_ap(fi, NDC * JBW)
                    for jci in range(NJC_IN):
                        jc = jb * NJC_IN + jci
                        bsel = jc % 2
                        ssel = jci
                        pb = (pM[:, bsel * 512:bsel * 512 + S] if half == 0
                              else pA[:, bsel * 512:bsel * 512 + S])
                        breg = (f"pM{bsel}" if half == 0 else f"pA{bsel}")
                        lm = None
                        for dc in range(NDC):
                            deps = []
                            if dc == 0:
                                deps = [lda, ufm_dep] + sc.wdeps(breg)
                            elif dc == half_dc:
                                deps = [ldb]
                            lm = sc.add(
                                "tensor",
                                lambda e, dc=dc, jci=jci, wap=wap, pb=pb:
                                e.matmul(
                                    pb,
                                    wap[:, dc * JBW + jci * 128:
                                        dc * JBW + (jci + 1) * 128],
                                    ufm[:, dc * S:(dc + 1) * S],
                                    start=(dc == 0), stop=(dc == NDC - 1)),
                                deps=deps,
                                sem=None if dc < NDC - 1 else "auto")
                        last_mm = lm
                        sc.wset(lm, breg)
                        if jci == NJC_IN - 1:
                            ring_cons[fi] = lm
                        if half == 0:
                            sg = sc.add("scalar",
                                        lambda e, pb=pb, ssel=ssel:
                                        e.activation(
                                            sig[:, ssel * S:(ssel + 1) * S],
                                            pb, AF.Sigmoid),
                                        deps=[lm] + sc.wdeps(f"sig{ssel}"))
                            sc.note_read(sg, breg)
                            sc.wset(sg, f"sig{ssel}")
                            sig_ops[jc] = sg
                        else:
                            hm = sc.add("vector",
                                        lambda e, pb=pb, jc=jc, ssel=ssel:
                                        e.tensor_tensor(
                                            hid[:, jc * S:(jc + 1) * S],
                                            sig[:, ssel * S:(ssel + 1) * S],
                                            pb, ALU.mult),
                                        deps=[sig_ops[jc], lm]
                                        + (sc.wdeps("hid") if jc == 0 else []))
                            sc.note_read(hm, breg, f"sig{ssel}")
                            last_hm = hm
            sc.note_read(last_mm, "ufm_w")
            sc.wset(last_hm, "hid")

            for eb in range(NEB):
                stop_mm = None
                if pump:
                    if eb == 0:
                        tail_drain_until(tail_marks["q3_end"])
                    tail_pump(pump)
                jc0 = 0
                for gsz in pb_parts:
                    if pump:
                        tail_pump(pump)
                    fi, ldopa, ldopb = ring_load(
                        q3v[:, jc0:jc0 + gsz, eb * EBW:(eb + 1) * EBW],
                        gsz * EBW, EBW,
                        extra_deps=[scratch_done["q3s"]], queue="scalar")
                    half_jc = (gsz + 1) // 2
                    wap = ring_ap(fi, gsz * EBW)
                    part_last = None
                    for jci in range(gsz):
                        jc = jc0 + jci
                        for tc in range(NTC):
                            first = (jc == 0)
                            lastg = (jc == NJC - 1)
                            deps = []
                            if first:
                                deps = sc.wdeps(f"pM{tc}")
                                if tc == 0:
                                    deps += [last_hm, ldopa]
                            elif jci == 0 and tc == 0:
                                deps = [ldopa]
                            elif jci == half_jc and tc == 0:
                                deps = [ldopb]
                            is_part_last = (jci == gsz - 1 and tc == NTC - 1)
                            mm = sc.add(
                                "tensor",
                                lambda e, jc=jc, tc=tc, jci=jci, wap=wap,
                                first=first, lastg=lastg:
                                e.matmul(
                                    pM[:, tc * 512:tc * 512 + EBW],
                                    hid[:, jc * S + tc * 128:
                                        jc * S + (tc + 1) * 128],
                                    wap[:, jci * EBW:(jci + 1) * EBW],
                                    start=first, stop=lastg),
                                deps=deps,
                                sem="auto" if is_part_last else None)
                            if is_part_last:
                                part_last = mm
                            if lastg and tc == NTC - 1:
                                stop_mm = mm
                    ring_cons[fi] = part_last
                    jc0 += gsz
                sc.note_read(stop_mm, "hid")
                if eb == 0 and ln_hook is not None:
                    ln_hook()
                    ln_hook = None
                for tc in range(NTC):
                    dst = xtile[:, tc * D + eb * EBW:tc * D + (eb + 1) * EBW]
                    ev = sc.add("vector",
                                lambda e, tc=tc, dst=dst:
                                e.scalar_tensor_tensor(
                                    dst, pM[:, tc * 512:tc * 512 + EBW], 0.5,
                                    dst, ALU.mult, ALU.add),
                                deps=[stop_mm] + sc.wdeps(xreg[tc]))
                    sc.note_read(ev, f"pM{tc}")
                    sc.wset(ev, xreg[tc])
                    if collect:
                        gdst = ubuf4[:, tc * D + eb * EBW:
                                     tc * D + (eb + 1) * EBW]
                        s2p = stats2[:, tc * 8 + 4 + eb:tc * 8 + 5 + eb]
                        s1p = stats2[:, tc * 8 + eb:tc * 8 + 1 + eb]
                        psq = sc.add("vector",
                                     lambda e, dst=dst, gdst=gdst, s2p=s2p:
                                     e.scalar_tensor_tensor(
                                         gdst, dst, 1.0, dst, ALU.mult,
                                         ALU.mult, accum_out=s2p),
                                     deps=[ev] + sc.wdeps(f"ub{tc}"))
                        sc.note_read(psq, xreg[tc])
                        sc.wset(psq, f"ub{tc}")
                        prd = sc.add("vector",
                                     lambda e, dst=dst, s1p=s1p:
                                     e.tensor_reduce(s1p, dst, AX, ALU.add),
                                     deps=[ev, psq])
                        sc.note_read(prd, xreg[tc])
                        stat_parts[tc].append(psq)
                        stat_parts[tc].append(prd)
            return

        def attn_ln_chain(tc, xtile, xreg, h_stop):
            asc = asc_slots[tc % 2]
            areg = f"asc{tc % 2}"
            hev = sc.add("scalar", lambda e, asc=asc: e.copy(asc[:], pM[:, 0:D]),
                         deps=[h_stop] + sc.wdeps(areg)
                         + (sc.wdeps("hid") if tc < 2 else []))
            if tc < 2:
                sc.note_read(hev, "hid")
            for b in range(NEB):
                sc.note_read(hev, f"pM{b}")
            sc.wset(hev, areg)
            slot = tc % NSLOT
            c0 = slot * 8
            s1 = stats[:, c0 + 0:c0 + 1]
            s2 = stats[:, c0 + 1:c0 + 2]
            nmu = stats[:, c0 + 2:c0 + 3]
            mu2 = stats[:, c0 + 3:c0 + 4]
            var = stats[:, c0 + 4:c0 + 5]
            stdv = stats[:, c0 + 5:c0 + 6]
            rstd = stats[:, c0 + 6:c0 + 7]
            sreg = f"stats{slot}"
            sgw = min(D, NJC_IN * S)
            sq = sc.add("vector", lambda e, sgw=sgw, asc=asc:
                e.scalar_tensor_tensor(
                    sig[:, 0:sgw], asc[:, 0:sgw], 1.0, asc[:, 0:sgw],
                    ALU.mult, ALU.mult, accum_out=s2),
                deps=[hev] + sc.wdeps(*[f"sig{i}" for i in range(NJC_IN)])
                + sc.wdeps(sreg))
            sc.note_read(sq, areg)
            assert D <= sgw, "attn square garbage dest too small"
            red = sc.add("vector", lambda e, asc=asc: e.tensor_reduce(
                s1, asc[:], AX, ALU.add), deps=[hev, sq])
            sc.note_read(red, areg)
            nm = sc.add("vector", lambda e: e.tensor_single_scalar(
                nmu, s1, -1.0 / D, ALU.mult), deps=[red])
            m2 = sc.add("vector", lambda e: e.tensor_mul(mu2, nmu, nmu),
                        deps=[nm])
            vv = sc.add("vector", lambda e: e.scalar_tensor_tensor(
                var, s2, 1.0 / D, mu2, ALU.mult, ALU.subtract),
                deps=[m2, sq])
            sd = sc.add("scalar", lambda e: e.activation(
                stdv, var, AF.Sqrt, bias=cEps[:, 0:1]),
                deps=[vv] + sc.rdeps("cEps"))
            rs = sc.add("vector", lambda e: e.reciprocal(rstd, stdv),
                        deps=[sd])
            for i in range(NJC_IN):
                sc.wset(sq, f"sig{i}")
            t1 = sc.add("vector", lambda e, asc=asc: e.scalar_tensor_tensor(
                asc[:], asc[:], nmu, Gbc[:], ALU.add, ALU.mult),
                deps=[nm, sq, red] + sc.rdeps("Gbc") + sc.wdeps(areg))
            sc.note_read(t1, "Gbc")
            sc.wset(t1, areg)
            t2 = sc.add("vector", lambda e, asc=asc: e.scalar_tensor_tensor(
                asc[:], asc[:], rstd, Bbc[:], ALU.mult, ALU.add),
                deps=[t1, rs] + sc.rdeps("Bbc"))
            sc.note_read(t2, "Bbc", sreg)
            sc.wset(t2, areg)
            sc.wset(rs, sreg)
            xa = xtile[:, tc * D:(tc + 1) * D]
            ra = sc.add("gpsimd", lambda e, xa=xa, asc=asc:
                        e.tensor_add(xa, xa, asc[:]),
                        deps=[t2] + sc.wdeps(xreg[tc]))
            sc.note_read(ra, areg)
            sc.wset(ra, xreg[tc])
            return ra

        def attn_sublayer(xtile, xreg, ufm_dep, reset_carry,
                          ln3_applies=None):
            if reset_carry:
                cr0 = sc.add("gpsimd", lambda e: e.memset(carry[:], 0.0),
                             deps=sc.wdeps("carry"))
                sc.wset(cr0, "carry")
                cr1 = sc.add("gpsimd", lambda e: e.memset(carry_bf[:], 0.0),
                             deps=sc.wdeps("carry_bf"))
                sc.wset(cr1, "carry_bf")

            last_zev = None
            last_stop = None
            for eq in range(NEB):
                fi, ldopa, ldopb = ring_load(
                    wtv[:, :, eq * EBW:(eq + 1) * EBW], NDC * EBW, EBW,
                    extra_deps=[scratch_done["wts"]])
                half_zdc = (NDC + 1) // 2
                wap = ring_ap(fi, NDC * EBW)
                stop_mms = [None] * NTC
                for dc in range(NDC):
                    for tc in range(NTC):
                        deps = []
                        if dc == 0:
                            deps = sc.wdeps(f"pM{tc}")
                            if tc == 0:
                                deps += [ldopa, ufm_dep]
                        elif dc == half_zdc and tc == 0:
                            deps = [ldopb]
                        mm = sc.add(
                            "tensor",
                            lambda e, dc=dc, tc=tc, wap=wap: e.matmul(
                                pM[:, tc * 512:tc * 512 + EBW],
                                ufm[:, dc * S + tc * 128:
                                    dc * S + (tc + 1) * 128],
                                wap[:, dc * EBW:(dc + 1) * EBW],
                                start=(dc == 0), stop=(dc == NDC - 1)),
                            deps=deps,
                            sem=None if dc < NDC - 1 else "auto")
                        if dc == NDC - 1:
                            stop_mms[tc] = mm
                ring_cons[fi] = stop_mms[NTC - 1]
                last_stop = stop_mms[NTC - 1]
                for tc in range(NTC):
                    zev = sc.add("scalar", lambda e, tc=tc, eq=eq:
                                 e.copy(
                                     z_sb(tc)[:, eq * EBW:(eq + 1) * EBW],
                                     pM[:, tc * 512:tc * 512 + EBW]),
                                 deps=[stop_mms[tc]]
                                 + (sc.wdeps("hid") if eq == 0 and tc == 0
                                    else []))
                    sc.note_read(zev, f"pM{tc}")
                    last_zev = zev
            sc.note_read(last_stop, "ufm_w")
            sc.wset(last_zev, "hid")
            z_dep = last_zev

            last_z_reader = None
            for tc in range(NTC):
                cu_prev = None
                for eb in range(NEB):
                    bsel = eb % 2
                    pca = pA[0:K, bsel * 512:bsel * 512 + EBW]
                    mmc = sc.add("tensor", lambda e, eb=eb, tc=tc, pca=pca:
                                 e.matmul(
                                     pca, CUb[:],
                                     z_sb(tc)[:, eb * EBW:(eb + 1) * EBW],
                                     start=True, stop=True),
                                 deps=[z_dep] + sc.rdeps("CUb")
                                 + sc.wdeps(f"pA{bsel}"))
                    sc.wset(mmc, f"pA{bsel}")
                    sc.note_read(mmc, "CUb")
                    cup = sc.add("vector",
                                 lambda e, eb=eb, pca=pca:
                                 e.scalar_tensor_tensor(
                                     carry[:, eb * EBW:(eb + 1) * EBW],
                                     carry[:, eb * EBW:(eb + 1) * EBW],
                                     lam128[:, 0:1], pca,
                                     ALU.mult, ALU.add),
                                 deps=[mmc] + sc.wdeps("carry")
                                 + sc.rdeps("lam128"))
                    sc.note_read(cup, f"pA{bsel}", "lam128")
                    sc.wset(cup, "carry")
                    cu_prev = cup
                last_z_reader = mmc

                for eb in range(NEB):
                    fd = sc.wdeps(f"pM{eb}")
                    if eb == 0:
                        fd += [z_dep] + sc.rdeps("LtT")
                    sc.add("tensor", lambda e, eb=eb, tc=tc: e.matmul(
                        pM[:, eb * 512:eb * 512 + EBW], LtT[:],
                        z_sb(tc)[:, eb * EBW:(eb + 1) * EBW],
                        start=True, stop=False),
                        deps=fd, sem=None)
                cp_deps = sc.rdeps("CPb", "carry_bf")
                h_stop = None
                for eb in range(NEB):
                    h_stop = sc.add("tensor", lambda e, eb=eb: e.matmul(
                        pM[:, eb * 512:eb * 512 + EBW], CPb[:],
                        carry_bf[:, eb * EBW:(eb + 1) * EBW],
                        start=False, stop=True),
                        deps=cp_deps if eb == 0 else [],
                        sem=None if eb < NEB - 1 else "auto")
                sc.note_read(h_stop, "carry_bf", "LtT", "CPb")
                for b in range(NEB):
                    sc.wset(h_stop, f"pM{b}")

                cbf = sc.add("gpsimd", lambda e: e.tensor_copy(
                    carry_bf[:], carry[:]),
                    deps=[cu_prev] + sc.wdeps("carry_bf"))
                sc.note_read(cbf, "carry")
                sc.wset(cbf, "carry_bf")

                attn_ln_chain(tc, xtile, xreg, h_stop)
                if ln3_applies is not None:
                    ln3_applies.append(emit_ln(
                        xtile[:, tc * D:(tc + 1) * D], tc % NSLOT,
                        xreg[tc], tc))
            sc.note_read(last_z_reader, "hid")

        xb1_addr = nc.lookup_mls("xb1").memorylocations[0].addr
        assert NTC * D * 4 >= 14 * D
        t_wb0 = nc.alloc_sbuf_tensor_at("t_wb0", [128, D], F32,
                                        offset=xb1_addr)
        t_wb1 = nc.alloc_sbuf_tensor_at("t_wb1", [128, D], F32,
                                        offset=xb1_addr + 4 * D)
        t_qa = nc.alloc_sbuf_tensor_at("t_qa", [128, D], F32,
                                       offset=xb1_addr + 8 * D)
        t_qc = nc.alloc_sbuf_tensor_at("t_qc", [128, D], BF16,
                                       offset=xb1_addr + 12 * D)
        RACCT = 40
        GT1, GT2 = 41, 42
        tail_done = []

        tail_thunks = []

        def build_tail_thunks():
            wbufs = [t_wb0, t_wb1]
            state = {"bf": [list(pro_done), list(pro_done)], "last_q": None,
                     "gamma_ready": None, "prev": None}
            tmats = [
                ("q3s", w3t_in, H, D, q3s, True, 2, "sg2"),
                ("wts", Wt_in, D, D, wts, False, -1, "sg3"),
            ]
            for (mname, w_ap, NR, NCOL, scratch, doq, rcol, sgkey) in tmats:
                nch = NR // 128
                if doq:
                    def t_z0():
                        z0 = sc.add("vector", lambda e: e.memset(
                            stats[:, RACCT:RACCT + 1], 0.0),
                            deps=list(pro_done))
                        state["prev"] = z0
                    tail_thunks.append(t_z0)
                    for c in range(nch):
                        def t_p1(c=c, w_ap=w_ap, NCOL=NCOL):
                            buf = wbufs[c % 2]
                            ld = sc.dma("sync", buf[:, 0:NCOL],
                                        w_ap[c * 128:(c + 1) * 128, :],
                                        deps=state["bf"][c % 2],
                                        key=f"tl{c % 2}")
                            red = sc.add("vector", lambda e, b=buf, n=NCOL:
                                         e.tensor_reduce(
                                             stats[:, GT1:GT1 + 1],
                                             b[:, 0:n], AX, ALU.add,
                                             apply_absolute_value=True),
                                         deps=[ld, state["prev"]])
                            state["prev"] = sc.add(
                                "vector", lambda e: e.tensor_add(
                                    stats[:, RACCT:RACCT + 1],
                                    stats[:, RACCT:RACCT + 1],
                                    stats[:, GT1:GT1 + 1]), deps=[red])
                            state["bf"][c % 2] = [red]
                        tail_thunks.append(t_p1)

                    def t_gam(NR=NR, NCOL=NCOL, rcol=rcol):
                        tg = sc.add("tensor", lambda e: e.matmul(
                            pA[0:1, 300:301], stats[:, RACCT:RACCT + 1],
                            onescolf[:], start=True, stop=True),
                            deps=[state["prev"]] + sc.wdeps("pA0"))
                        sc.wset(tg, "pA0")
                        gm = sc.add("vector", lambda e, n=NR * NCOL:
                                    e.tensor_scalar(
                                        stats[0:1, GT1:GT1 + 1],
                                        pA[0:1, 300:301], 1.0 / n, Q_EPS,
                                        ALU.mult, ALU.add), deps=[tg])
                        sc.note_read(gm, "pA0")
                        rc = sc.add("vector", lambda e: e.reciprocal(
                            stats[0:1, GT2:GT2 + 1],
                            stats[0:1, GT1:GT1 + 1]), deps=[gm])
                        tb = sc.add("tensor", lambda e: e.matmul(
                            pA[:, 301:302], ones1f[:],
                            stats[0:1, GT2:GT2 + 1],
                            start=True, stop=True),
                            deps=[rc] + sc.wdeps("pA0"))
                        sc.wset(tb, "pA0")
                        gr = sc.add("vector", lambda e, rcol=rcol:
                                    e.tensor_copy(
                                        recip[:, rcol:rcol + 1],
                                        pA[:, 301:302]), deps=[tb])
                        sc.note_read(gr, "pA0")
                        state["gamma_ready"] = gr
                    tail_thunks.append(t_gam)

                for c in range(nch):
                    def t_p2(c=c, w_ap=w_ap, NCOL=NCOL, doq=doq, rcol=rcol,
                             scratch=scratch, sgkey=sgkey, mname=mname,
                             nch=nch):
                        buf = wbufs[c % 2]
                        ld = sc.dma("sync", buf[:, 0:NCOL],
                                    w_ap[c * 128:(c + 1) * 128, :],
                                    deps=state["bf"][c % 2],
                                    key=f"tl{c % 2}")
                        if doq:
                            qa_ = sc.add("vector",
                                         lambda e, b=buf, n=NCOL, rcol=rcol:
                                         e.tensor_scalar(
                                             t_qa[:, 0:n], b[:, 0:n],
                                             recip[:, rcol:rcol + 1], 0.5,
                                             ALU.mult, ALU.is_ge),
                                         deps=[ld, state["gamma_ready"],
                                               state["last_q"]])
                            qb_ = sc.add("vector",
                                         lambda e, b=buf, n=NCOL, rcol=rcol:
                                         e.tensor_scalar(
                                             t_qc[:, 0:n], b[:, 0:n],
                                             recip[:, rcol:rcol + 1], -0.5,
                                             ALU.mult, ALU.is_le),
                                         deps=[ld, state["gamma_ready"],
                                               state["last_q"]])
                            q_ = sc.add("vector", lambda e, n=NCOL:
                                        e.tensor_tensor(
                                            t_qc[:, 0:n], t_qa[:, 0:n],
                                            t_qc[:, 0:n], ALU.subtract),
                                        deps=[qa_, qb_])
                        else:
                            q_ = sc.add("vector", lambda e, b=buf, n=NCOL:
                                        e.tensor_copy(t_qc[:, 0:n],
                                                      b[:, 0:n]),
                                        deps=[ld, state["last_q"]])
                        state["bf"][c % 2] = [q_]
                        st_op = sc.dma("sync",
                                       scratch[c * 128:(c + 1) * 128, :],
                                       t_qc[:, 0:NCOL],
                                       deps=[q_], key=sgkey)
                        state["last_q"] = st_op
                        scratch_done[mname] = st_op
                        if mname == "wts" and c == nch - 1:
                            tail_done.append(st_op)
                    tail_thunks.append(t_p2)
                if mname == "q3s":
                    tail_marks["q3_end"] = len(tail_thunks)

        tail_marks = {}
        build_tail_thunks()
        tail_state = {"emitted": 0}

        def tail_pump(n):
            k = 0
            while tail_thunks and k < n:
                tail_thunks.pop(0)()
                tail_state["emitted"] += 1
                k += 1

        def tail_drain_until(n):
            while tail_thunks and tail_state["emitted"] < n:
                tail_thunks.pop(0)()
                tail_state["emitted"] += 1

        def tail_drain():
            while tail_thunks:
                tail_thunks.pop(0)()
                tail_state["emitted"] += 1

        last_store = None
        y_stores = [None, None]
        x_dep = [None, None]
        pend_ln = [None]

        x_dep[0] = x0_ld
        pend_ln[0] = emit_ln_stats_all(xb[0],
                                       [f"xb0_{t}" for t in range(NTC)],
                                       extra_dep=x0_ld)
        ev0 = emit_transposes_all(pend_ln[0])

        for g in range(NTILE):
            si, ti = divmod(g, NT_SEQ)
            p = g % 2
            t0 = ti * S
            xtile = xb[p]
            xreg = [f"xb{p}_{tc}" for tc in range(NTC)]
            def emit_next_load(g=g):
                si2, ti2 = divmod(g + 1, NT_SEQ)
                pn = (g + 1) % 2
                ld = sc.dma("scalar",
                            xb[pn][:].rearrange("p (tc d) -> p tc d", d=D),
                            x_in[si2, ti2 * S:ti2 * S + S, :].rearrange(
                                "(tc q) d -> q tc d", q=128),
                            deps=sc.wdeps(
                                *[f"xb{pn}_{t}" for t in range(NTC)])
                            + (tail_done if g == 0 else []),
                            key=f"xl{pn}")
                x_dep[pn] = ld
                sc.wset(ld, *[f"xb{pn}_{t}" for t in range(NTC)])

            if g + 1 < NTILE and g != 0:
                emit_next_load()

            if g == 0:
                ev = ev0
            else:
                ev = emit_transposes_all(pend_ln[0])
            stat_parts = [[] for _ in range(NTC)]
            ffn_sublayer(xtile, xreg, ev, collect=True, stat_parts=stat_parts,
                         pump=5 if g == 0 else 0)
            applies = [emit_ln_fin(xtile, xreg, tc, stat_parts[tc])
                       for tc in range(NTC)]
            ev = emit_transposes_all(applies)
            ln3_applies = []
            if g == 0:
                tail_drain()
                if NTILE > 1:
                    emit_next_load()
            attn_sublayer(xtile, xreg, ev, reset_carry=(ti == 0 and g > 0),
                          ln3_applies=ln3_applies)
            ev = emit_transposes_all(ln3_applies)
            if g + 1 < NTILE:
                pn2 = (g + 1) % 2
                xreg_n = [f"xb{pn2}_{t}" for t in range(NTC)]

                def ln_hook(pn2=pn2, xreg_n=xreg_n):
                    pend_ln[0] = emit_ln_stats_all(xb[pn2], xreg_n,
                                                   extra_dep=x_dep[pn2])
                ffn_sublayer(xtile, xreg, ev, ln_hook=ln_hook)
            else:
                ffn_sublayer(xtile, xreg, ev)

            st = sc.dma("scalar",
                        y_out[si, t0:t0 + S, :].rearrange(
                            "(tc q) d -> q tc d", q=128),
                        xtile[:].rearrange("p (tc d) -> p tc d", d=D),
                        deps=sc.rdeps(*xreg), key=f"ys{p}")
            sc.note_read(st, *xreg)
            y_stores[p] = st
            last_store = st

        sc.wait_only("scalar", [s for s in y_stores if s is not None])

        sc.finalize()
        with nc.Block() as block:
            sc.emit_all(block)

    return {e: sum(1 for o in sc.ops if o.engine == e) for e in ENGINES}


def kernel(**kw):
    x = np.asarray(kw["L0_attn_x"], dtype=np.float32)
    W = np.ascontiguousarray(np.asarray(kw["L0_attn_W"], dtype=np.float32))
    lam_logit = np.ascontiguousarray(
        np.asarray(kw["L0_attn_lam_logit"], dtype=np.float32))
    eta = np.ascontiguousarray(np.asarray(kw["L0_attn_eta"], dtype=np.float32))
    gamma = np.ascontiguousarray(
        np.asarray(kw["L0_attn_gamma"], dtype=np.float32))
    beta = np.ascontiguousarray(
        np.asarray(kw["L0_attn_beta"], dtype=np.float32))
    w1t = np.ascontiguousarray(np.asarray(kw["L0_ffn_w1"], dtype=np.float32).T)
    w2t = np.ascontiguousarray(np.asarray(kw["L0_ffn_w2"], dtype=np.float32).T)
    w3t = np.ascontiguousarray(np.asarray(kw["L0_ffn_w3"], dtype=np.float32).T)
    Wt = np.ascontiguousarray(W.T)

    B = x.shape[0]
    n_cores = 8
    bl = B // n_cores
    cfg = dict(CFG_FULL)
    cfg.update(B_loc=bl, T=x.shape[1], D=x.shape[2], H=w1t.shape[1])

    nc = bass.Bass("TRN2", target_bir_lowering=False)
    build_program(nc, cfg)
    nc.finalize()

    in_maps = []
    for c in range(n_cores):
        in_maps.append({
            "x": np.ascontiguousarray(x[c * bl:(c + 1) * bl]),
            "Wt": Wt, "lam_logit": lam_logit, "eta": eta,
            "gamma": gamma, "beta": beta,
            "w1t": w1t, "w2t": w2t, "w3t": w3t,
        })
    res = run_bass_kernel_spmd(nc, in_maps, core_ids=list(range(n_cores)))
    return np.concatenate([np.asarray(r["y"]) for r in res.results], axis=0)
